# version 08
# baseline (speedup 1.0000x reference)
import numpy as np
import ml_dtypes
import concourse.bass as bass
import concourse.mybir as mybir
from concourse.bass_utils import run_bass_kernel_spmd
from concourse.aot_env import allow_ici_all_to_all
from contextlib import ExitStack

F32 = mybir.dt.float32
BF16 = mybir.dt.bfloat16
I32 = mybir.dt.int32
AF = mybir.ActivationFunctionType
ALU = mybir.AluOpType
AX = mybir.AxisListType.X

NCORES = 8
HID = 4096
SEQ = 4096
NH_SP = 4
NBLK = 16
CH = 32
EPS_ATTN = 1e-5
EPS_LN = 1e-5
GROUPS_BATCH = [[0, 1, 2, 3], [4, 5, 6, 7]]
GROUPS_ALL = [list(range(NCORES))]

SEM_NAMES = ("ldx ldq mm ev tp ktd vt sc dk im umm isc ae kv kvp ast cc init xc "
             "lat ldw bmm sig gn bt bte bts ye sq ln misc xr stx out ldb arw nrm").split()


class Sem:
    def __init__(self, ctx, nc, name):
        self.h = ctx.enter_context(nc.semaphore(name))
        self.n = 0

    def inc(self, instr, k=1):
        instr.then_inc(self.h, k)
        self.n += k


class Sems:
    def __init__(self, ctx, nc):
        for nm in SEM_NAMES:
            setattr(self, nm, Sem(ctx, nc, "s_" + nm))


def build_nc(stage="full", nblk=NBLK, dbg_disable=(), bstop=99):
    nc = bass.Bass("TRN2", target_bir_lowering=False, debug=False)

    xT_b = nc.dram_tensor("xT_b", [HID, SEQ], F32, kind="ExternalInput").ap()
    xTloc = nc.dram_tensor("xTloc", [HID, 1024], F32, kind="ExternalInput").ap()
    x_rows = nc.dram_tensor("x_rows", [1024, HID], F32, kind="ExternalInput").ap()
    qkvwT = nc.dram_tensor("qkvwT", [HID, 3072], F32, kind="ExternalInput").ap()
    gatewT = nc.dram_tensor("gatewT", [16, 128, CH, 256], F32, kind="ExternalInput").ap()
    outwT = nc.dram_tensor("outwT", [16, 128, CH, 256], F32, kind="ExternalInput").ap()
    WT = nc.dram_tensor("WT", [16, 128, CH, 256], F32, kind="ExternalInput").ap()
    Amat = nc.dram_tensor("Amat", [HID, 32], F32, kind="ExternalInput").ap()
    Bm = nc.dram_tensor("Bm", [33, HID], F32, kind="ExternalInput").ap()
    limes = nc.dram_tensor("limes", [8, 32], F32, kind="ExternalInput").ap()
    limesT = nc.dram_tensor("limesT", [32, 8], F32, kind="ExternalInput").ap()
    nwcol = nc.dram_tensor("nwcol", [128, 32], F32, kind="ExternalInput").ap()
    b8 = nc.dram_tensor("b8", [8, 1], F32, kind="ExternalInput").ap()
    diag = nc.dram_tensor("diag", [128, 8, 2, 256], F32, kind="ExternalInput").ap()
    qdec = nc.dram_tensor("qdec", [128, 16], F32, kind="ExternalInput").ap()
    kdec = nc.dram_tensor("kdec", [128, 16], F32, kind="ExternalInput").ap()
    bdec = nc.dram_tensor("bdec", [128, 8], F32, kind="ExternalInput").ap()
    out_e = nc.dram_tensor("out", [1024, HID], F32, kind="ExternalOutput").ap()

    a2a_in = [nc.dram_tensor(f"a2a_in{i}", [4096, 512], BF16, kind="Internal") for i in range(2)]
    a2a_out = [nc.dram_tensor(f"a2a_out{i}", [4096, 512], BF16, kind="Internal") for i in range(2)]
    ar_in = nc.dram_tensor("ar_in", [1, 16], F32, kind="Internal")
    ar_out = nc.dram_tensor("ar_out", [1, 16], F32, kind="Internal")
    x2stash = nc.dram_tensor("x2stash", [512, HID], F32, kind="Internal")

    with ExitStack() as top:
        E = top.enter_context
        S = Sems(top, nc)

        def dmaw(eng, sem, out, in_, **kw):
            eng.wait_ge(sem.h, sem.n)
            sem.inc(eng.dma_start(out=out, in_=in_, **kw), 16)

        ps_q0 = E(nc.psum_tensor("ps_q0", [128, 512], F32))
        ps_q1 = E(nc.psum_tensor("ps_q1", [128, 512], F32))
        PSQ = [ps_q0, ps_q1]
        pa_psum = ExitStack()
        ps_sb = [pa_psum.enter_context(nc.psum_tensor(f"ps_s{i}", [128, 512], F32)) for i in range(2)]
        ps_in = [pa_psum.enter_context(nc.psum_tensor(f"ps_in{i}", [128, 512], F32)) for i in range(2)]
        ps_tb = [pa_psum.enter_context(nc.psum_tensor(f"ps_t{i}", [128, 1024], BF16)) for i in range(2)]

        block = E(nc.Block())

        ident_i = E(nc.sbuf_tensor("ident_i", [128, 128], I32))
        idb = E(nc.sbuf_tensor("idb", [128, 128], BF16))
        idf = E(nc.sbuf_tensor("idf", [128, 128], F32))
        onesf = E(nc.sbuf_tensor("onesf", [1, 128], F32))
        ones_bf = E(nc.sbuf_tensor("ones_bf", [33, 128], BF16))
        zero16 = E(nc.sbuf_tensor("zero16", [1, 16], F32))
        cst_eps = E(nc.sbuf_tensor("cst_eps", [128, 1], F32))
        nc.const_aps.register(F32, 1e-05, cst_eps[:, :])

        @block.gpsimd
        def _(g):
            g.iota(ident_i[:, :], pattern=[[1, 128]], base=0, channel_multiplier=-1)
            g.memset(onesf[:, :], 1.0)
            g.memset(ones_bf[:, :], 1.0)
            g.memset(zero16[:, :], 0.0)
            i = g.memset(cst_eps[:, :], 1e-05)
            S.init.inc(i, 1)

        @block.vector
        def _(v):
            v.wait_ge(S.init.h, 1)
            v.tensor_scalar(idb[:, :], ident_i[:, :], 0.0, None, ALU.is_equal)
            i = v.tensor_scalar(idf[:, :], ident_i[:, :], 0.0, None, ALU.is_equal)
            S.misc.inc(i, 1)

        @block.sync
        def _(sy):
            sy.wait_ge(S.misc.h, S.misc.n)
            dmaw(sy, S.arw, ar_in.ap()[:, :], zero16[:, :])
        ar_zero_done = S.arw.n

        pa = ExitStack()
        EA = pa.enter_context
        qw_sb = EA(nc.sbuf_tensor("qw_sb", [128, CH, 1536], BF16))
        xts = [EA(nc.sbuf_tensor(f"xts{i}", [128, CH, 512], BF16)) for i in range(2)]
        qkv_sb = EA(nc.sbuf_tensor("qkv_sb", [128, 12, 512], BF16))
        ktd_sb = EA(nc.sbuf_tensor("ktd_sb", [128, 2, 2, 128], BF16))
        vt_sb = EA(nc.sbuf_tensor("vt_sb", [128, 2, 2, 128], BF16))
        sc_bf = EA(nc.sbuf_tensor("sc_bf", [128, 2, 3, 128], BF16))
        tmp_sc = EA(nc.sbuf_tensor("tmp_sc", [128, 2, 256], F32))
        kv_f = EA(nc.sbuf_tensor("kv_f", [128, NH_SP, 128], F32))
        kv_bf = EA(nc.sbuf_tensor("kv_bf", [128, NH_SP, 128], BF16))
        att_sb = [EA(nc.sbuf_tensor(f"att_sb{i}", [128, 2, 512], BF16)) for i in range(2)]
        diag_sb = EA(nc.sbuf_tensor("diag_sb", [128, 8, 2, 256], F32))
        qdec_sb = EA(nc.sbuf_tensor("qdec_sb", [128, 16], F32))
        kdec_sb = EA(nc.sbuf_tensor("kdec_sb", [128, 16], F32))
        bdec_sb = EA(nc.sbuf_tensor("bdec_sb", [128, 8], F32))

        qkvwT3 = qkvwT.rearrange("(c p) n -> p c n", p=128)
        xT3 = xT_b.rearrange("(c p) t -> p c t", p=128)

        @block.gpsimd
        def _(g):
            dmaw(g, S.ldb, diag_sb[:, :, :, :], diag[:, :, :, :])
            dmaw(g, S.ldb, qdec_sb[:, :], qdec[:, :])
            dmaw(g, S.ldb, kdec_sb[:, :], kdec[:, :])
            dmaw(g, S.ldb, bdec_sb[:, :], bdec[:, :])
        decays_done = S.ldb.n

        silu_per_bank = [0, 0]
        uc = [0]
        unit_p = {}
        t_guard = {}
        s_guard = {}
        in_guard = {}
        kv_hs = {}

        for sp in range(2):
            @block.gpsimd
            def _(g, sp=sp):
                if sp == 1:
                    g.wait_ge(S.mm.h, S.mm.n)
                dmaw(g, S.ldq, qw_sb[:, :, :], qkvwT3[:, :, sp * 1536:(sp + 1) * 1536], max_dma_last_dim=6144)

            @block.vector
            def _(v, sp=sp):
                if sp == 1:
                    v.wait_ge(S.im.h, S.im.n)
                else:
                    v.wait_ge(S.ldb.h, decays_done)
                v.memset(kv_f[:, :, :], 0.0)
                i = v.memset(kv_bf[:, :, :], 0.0)
                v.drain()
                S.kv.inc(i, 1)
            for _hs in range(NH_SP):
                kv_hs[_hs] = S.kv.n
            if sp == 1:
                @block.gpsimd
                def _(g):
                    g.wait_ge(S.ast.h, S.ast.n)
                    with allow_ici_all_to_all():
                        i = g.collective_compute(
                            "AllToAll", ALU.bypass,
                            replica_groups=GROUPS_ALL,
                            ins=[a2a_in[0].ap().opt()],
                            outs=[a2a_out[0].ap().opt()],
                        )
                    S.cc.inc(i, 1)

            for tl in range(nblk // 2):
                ti = sp * (nblk // 2) + tl

                @block.gpsimd
                def _(g, tl=tl, ti=ti, buf=ti % 2):
                    if ti >= 2:
                        g.wait_ge(S.mm.h, 12 * (ti - 1))
                    dmaw(g, S.ldx, xts[buf][:, :, :],
                         xT3[:, :, tl * 512:(tl + 1) * 512], max_dma_last_dim=2048)

                for ct in range(12):
                    bank = ct % 2
                    ps = PSQ[bank]
                    guard = silu_per_bank[bank]

                    @block.tensor
                    def _(t, ct=ct, buf=ti % 2, ti=ti, sp=sp, ps=ps, guard=guard):
                        if ct == 0:
                            if ti == 0 and sp == 0:
                                t.wait_ge(S.misc.h, 1)
                            t.wait_ge(S.ldx.h, 16 * (ti + 1))
                            t.wait_ge(S.ldq.h, 16 * (sp + 1))
                        if guard > 0:
                            t.wait_ge(S.ev.h, guard)
                        for ch in range(CH):
                            i = t.matmul(ps[:, 0:512],
                                         qw_sb[:, ch, ct * 128:(ct + 1) * 128],
                                         xts[buf][:, ch, :],
                                         start=(ch == 0), stop=(ch == CH - 1))
                        S.mm.inc(i, 1)

                    @block.scalar
                    def _(sx, ct=ct, ps=ps, first=(ct == 0 and ti == 0)):
                        sx.wait_ge(S.mm.h, S.mm.n)
                        if first:
                            sx.wait_ge(S.ldb.h, decays_done)
                        i = sx.activation(qkv_sb[:, ct, :], ps[:, 0:512], AF.Silu)
                        S.ev.inc(i, 1)
                    silu_per_bank[bank] = S.ev.n

                units = [(bb, hs) for bb in range(2) for hs in range(NH_SP)]
                ae_blk = [0, 0]

                def stage1(un):
                    bb, hs = un
                    hh = sp * NH_SP + hs
                    a_seq = 2 * tl + bb
                    last_blk = (a_seq == nblk - 1)
                    p = uc[0] % 2
                    unit_p[un] = p
                    o = bb * 256
                    q_ap = qkv_sb[:, 3 * hs + 0, o:o + 256]
                    k_ap = qkv_sb[:, 3 * hs + 1, o:o + 256]
                    v_ap = qkv_sb[:, 3 * hs + 2, o:o + 256]
                    pt = ps_tb[p]
                    pss = ps_sb[p]

                    @block.tensor
                    def _(t, hs=hs, p=p, last_blk=last_blk, k_ap=k_ap, q_ap=q_ap,
                          v_ap=v_ap, pt=pt, pss=pss,
                          tg=t_guard.get(p), sg=s_guard.get(p)):
                        t.wait_ge(S.ev.h, S.ev.n - (11 - (3 * hs + 2)))
                        if tg is not None:
                            for sem, val in tg:
                                t.wait_ge(sem.h, val)
                        i = t.transpose(pt[:, 0:128], v_ap[:, 0:128], idb[:, :])
                        i = t.transpose(pt[:, 128:256], v_ap[:, 128:256], idb[:, :])
                        if not last_blk:
                            i = t.transpose(pt[:, 256:384], k_ap[:, 0:128], idb[:, :])
                            i = t.transpose(pt[:, 384:512], k_ap[:, 128:256], idb[:, :])
                        S.tp.inc(i, 1)
                        if sg is not None:
                            for sem, val in sg:
                                t.wait_ge(sem.h, val)
                        for j, (kc, qc) in enumerate([(0, 0), (0, 1), (1, 1)]):
                            i = t.matmul(pss[:, j * 128:(j + 1) * 128],
                                         k_ap[:, kc * 128:(kc + 1) * 128],
                                         q_ap[:, qc * 128:(qc + 1) * 128],
                                         start=True, stop=True)
                        S.sc.inc(i, 1)

                    @block.vector
                    def _(v, p=p, pt=pt):
                        v.wait_ge(S.tp.h, S.tp.n)
                        v.tensor_copy(vt_sb[:, p, 0, :], pt[:, 0:128])
                        i = v.tensor_copy(vt_sb[:, p, 1, :], pt[:, 128:256])
                        S.vt.inc(i, 1)

                    if not last_blk:
                        @block.scalar
                        def _(sx, hh=hh, p=p, pt=pt):
                            sx.wait_ge(S.tp.h, S.tp.n)
                            sx.activation(ktd_sb[:, p, 0, :], pt[:, 256:384], AF.Copy,
                                          scale=kdec_sb[:, 2 * hh + 0:2 * hh + 1])
                            i = sx.activation(ktd_sb[:, p, 1, :], pt[:, 384:512], AF.Copy,
                                              scale=kdec_sb[:, 2 * hh + 1:2 * hh + 2])
                            S.ktd.inc(i, 1)

                    @block.vector
                    def _(v, hh=hh, p=p, pss=pss):
                        v.wait_ge(S.sc.h, S.sc.n)
                        for j, (kc, qc) in enumerate([(0, 0), (0, 1), (1, 1)]):
                            i = v.tensor_tensor(
                                sc_bf[:, p, j, :], pss[:, j * 128:(j + 1) * 128],
                                diag_sb[:, hh, kc, qc * 128:(qc + 1) * 128], ALU.mult)
                        S.dk.inc(i, 1)
                    t_guard[p] = [(S.vt, S.vt.n)] + ([(S.ktd, S.ktd.n)] if not last_blk else [])
                    uc[0] += 1

                def stage2(un):
                    bb, hs = un
                    hh = sp * NH_SP + hs
                    a_seq = 2 * tl + bb
                    gb = sp * nblk + a_seq
                    last_blk = (a_seq == nblk - 1)
                    p = unit_p[un]
                    o = bb * 256
                    q_ap = qkv_sb[:, 3 * hs + 0, o:o + 256]
                    pss = ps_sb[p]
                    pin = ps_in[p]
                    blkbuf = a_seq % 2

                    @block.tensor
                    def _(t, hs=hs, p=p, q_ap=q_ap, pss=pss, pin=pin,
                          last_blk=last_blk, ig=in_guard.get(p),
                          kvw=kv_hs.get(hs, 0)):
                        t.wait_ge(S.dk.h, S.dk.n)
                        t.wait_ge(S.vt.h, S.vt.n)
                        if ig is not None:
                            for sem, val in ig:
                                t.wait_ge(sem.h, val)
                        t.matmul(pin[:, 0:128], sc_bf[:, p, 0, :], vt_sb[:, p, 0, :],
                                 start=True, stop=True)
                        t.matmul(pin[:, 128:256], sc_bf[:, p, 1, :], vt_sb[:, p, 0, :],
                                 start=True, stop=False)
                        t.matmul(pin[:, 128:256], sc_bf[:, p, 2, :], vt_sb[:, p, 1, :],
                                 start=False, stop=True)
                        t.wait_ge(S.kv.h, kvw)
                        t.matmul(pin[:, 256:384], q_ap[:, 0:128], kv_bf[:, hs, :],
                                 start=True, stop=True)
                        i = t.matmul(pin[:, 384:512], q_ap[:, 128:256], kv_bf[:, hs, :],
                                     start=True, stop=True)
                        S.im.inc(i, 1)
                        if not last_blk:
                            t.wait_ge(S.ktd.h, S.ktd.n)
                            t.matmul(pss[:, 384:512], ktd_sb[:, p, 0, :], vt_sb[:, p, 0, :],
                                     start=True, stop=False)
                            i = t.matmul(pss[:, 384:512], ktd_sb[:, p, 1, :], vt_sb[:, p, 1, :],
                                         start=False, stop=True)
                            S.umm.inc(i, 1)

                    @block.scalar
                    def _(sx, hh=hh, p=p, pin=pin):
                        sx.wait_ge(S.im.h, S.im.n)
                        for qc in range(2):
                            i = sx.activation(tmp_sc[:, p, qc * 128:(qc + 1) * 128],
                                              pin[:, 256 + qc * 128:256 + (qc + 1) * 128],
                                              AF.Copy,
                                              scale=qdec_sb[:, 2 * hh + qc:2 * hh + qc + 1])
                        S.isc.inc(i, 1)

                    @block.vector
                    def _(v, hs=hs, p=p, blkbuf=blkbuf, gb=gb, pin=pin):
                        v.wait_ge(S.isc.h, S.isc.n)
                        if gb >= 2:
                            v.wait_ge(S.ast.h, 32 * (gb - 1))
                        for qc in range(2):
                            i = v.tensor_tensor(
                                att_sb[blkbuf][:, qc, hs * 128:(hs + 1) * 128],
                                pin[:, qc * 128:(qc + 1) * 128],
                                tmp_sc[:, p, qc * 128:(qc + 1) * 128], ALU.add)
                        S.ae.inc(i, 1)

                    if not last_blk:
                        @block.vector
                        def _(v, hs=hs, hh=hh, p=p, pss=pss):
                            v.wait_ge(S.umm.h, S.umm.n)
                            i = v.scalar_tensor_tensor(
                                kv_f[:, hs, :], kv_f[:, hs, :],
                                bdec_sb[:, hh:hh + 1], pss[:, 384:512],
                                op0=ALU.mult, op1=ALU.add)
                            S.kvp.inc(i, 1)

                        @block.scalar
                        def _(sx, hs=hs):
                            sx.wait_ge(S.kvp.h, S.kvp.n)
                            i = sx.activation(kv_bf[:, hs, :], kv_f[:, hs, :], AF.Copy)
                            S.kv.inc(i, 1)
                        kv_hs[hs] = S.kv.n
                        s_guard[p] = [(S.dk, S.dk.n), (S.kvp, S.kvp.n)]
                    else:
                        s_guard[p] = [(S.dk, S.dk.n), (S.im, S.im.n)]
                    in_guard[p] = [(S.ae, S.ae.n), (S.isc, S.isc.n)]
                    if hs == NH_SP - 1:
                        ae_blk[bb] = S.ae.n

                for step in range(len(units) + 1):
                    if step < len(units):
                        stage1(units[step])
                    if step >= 1:
                        stage2(units[step - 1])

                for bb in range(2):
                    a_seq = 2 * tl + bb

                    @block.sync
                    def _(sy, sp=sp, a_seq=a_seq, blkbuf=a_seq % 2, ae_need=ae_blk[bb]):
                        sy.wait_ge(S.ae.h, ae_need)
                        for half in range(2):
                            dmaw(sy, S.ast,
                                 a2a_in[sp].ap()[a_seq * 256 + half * 128:a_seq * 256 + half * 128 + 128, :],
                                 att_sb[blkbuf][:, half, :])

        @block.gpsimd
        def _(g):
            g.wait_ge(S.ast.h, S.ast.n)
            with allow_ici_all_to_all():
                i = g.collective_compute(
                    "AllToAll", ALU.bypass,
                    replica_groups=GROUPS_ALL,
                    ins=[a2a_in[1].ap().opt()],
                    outs=[a2a_out[1].ap().opt()],
                )
            S.cc.inc(i, 1)
        phase_a_ast = S.ast.n
        qkv_all = S.mm.n

        pa.close()
        pa_psum.close()
        pb_psum = ExitStack()
        px = [pb_psum.enter_context(nc.psum_tensor(f"ps_x{i}", [128, 512], F32)) for i in range(2)]
        pn = pb_psum.enter_context(nc.psum_tensor("ps_pn", [128, 512], F32))

        a2a_r = [a2a_out[i].ap().rearrange("(s t) f -> t s f", s=8) for i in range(2)]
        Amat3 = Amat.rearrange("(c p) n -> p c n", p=128)
        xTl3 = xTloc.rearrange("(c p) t -> p c t", p=128)

        if stage == "attn":
            pb = ExitStack()
            EB = pb.enter_context
            at_t = EB(nc.sbuf_tensor("at_t", [128, 4, 1024], BF16))
            at_f = EB(nc.sbuf_tensor("at_f", [128, 4096], F32))
            for tt in range(8):
                @block.sync
                def _(sy, tt=tt):
                    sy.wait_ge(S.cc.h, 2)
                    if tt >= 1:
                        sy.wait_ge(S.out.h, 16 * tt)
                    cbd, ttd = tt // 4, tt % 4
                    dmaw(sy, S.lat, at_t[:, :, :], a2a_r[ttd * 128:(ttd + 1) * 128, 4 * cbd:4 * cbd + 4, :])

                @block.vector
                def _(v):
                    v.wait_ge(S.lat.h, S.lat.n)
                    i = v.tensor_copy(at_f[:, :], at_t.rearrange("p s f -> p (s f)")[:, :])
                    S.misc.inc(i, 1)

                @block.sync
                def _(sy, tt=tt):
                    sy.wait_ge(S.misc.h, S.misc.n)
                    dmaw(sy, S.out, out_e[tt * 128:(tt + 1) * 128, :], at_f[:, :])

            @block.sync
            def _(sy):
                sy.wait_ge(S.out.h, S.out.n)
            pb.close()
            pb_psum.close()
            return nc

        pb = ExitStack()
        EB = pb.enter_context
        big1 = EB(nc.sbuf_tensor("big1", [128, 16384], BF16))
        at_v = lambda tt: big1[:, tt * 4096:(tt + 1) * 4096]
        gnT_v = lambda fc, t0, t1: big1[:, fc * 512 + t0:fc * 512 + t1]
        xx_sb = EB(nc.sbuf_tensor("xx_sb", [128, CH, 512], BF16))
        u1 = EB(nc.sbuf_tensor("u1", [128, HID], F32))
        wbuf = [EB(nc.sbuf_tensor(f"wbuf{i}", [128, CH, 256], BF16)) for i in range(2)]
        BmB = EB(nc.sbuf_tensor("BmB", [33, HID], BF16))
        nw_sb = EB(nc.sbuf_tensor("nw_sb", [128, 32], F32))
        b8_sb = EB(nc.sbuf_tensor("b8_sb", [8, 1], F32))
        A_bf = EB(nc.sbuf_tensor("A_bf", [128, CH, 32], BF16))
        W8_bf = EB(nc.sbuf_tensor("W8_bf", [128, CH, 8], BF16))
        lim_bf = EB(nc.sbuf_tensor("lim_bf", [8, 32], BF16))
        limT_bf = EB(nc.sbuf_tensor("limT_bf", [32, 8], BF16))
        zT_sb = [EB(nc.sbuf_tensor(f"zT_sb{i}", [32, 512], BF16)) for i in range(2)]
        zpT_sb = [EB(nc.sbuf_tensor(f"zpT_sb{i}", [32, 512], BF16)) for i in range(2)]
        hsA = [EB(nc.sbuf_tensor(f"hsA{i}", [8, 256], F32)) for i in range(2)]
        dsA = [EB(nc.sbuf_tensor(f"dsA{i}", [8, 256], F32)) for i in range(2)]
        mx8 = EB(nc.sbuf_tensor("mx8", [8, 4], F32))
        mx2 = EB(nc.sbuf_tensor("mx2", [8, 2], F32))
        arv = EB(nc.sbuf_tensor("arv", [2, 8], F32))
        ar2 = EB(nc.sbuf_tensor("ar2", [2, 1], F32))
        arr = EB(nc.sbuf_tensor("arr", [1, 16], F32))
        inv2 = EB(nc.sbuf_tensor("inv2", [1, 2], F32))
        invb = EB(nc.sbuf_tensor("invb", [128, 2], F32))
        stats = EB(nc.sbuf_tensor("stats", [128, 32], F32))
        s2buf = EB(nc.sbuf_tensor("s2buf", [128, 64], F32))
        q2buf = EB(nc.sbuf_tensor("q2buf", [128, 64], F32))
        dump = EB(nc.sbuf_tensor("dump", [128, 256], BF16))
        wfT = [EB(nc.sbuf_tensor(f"wfT{i}", [8, 512], BF16)) for i in range(2)]
        hs_tm = EB(nc.sbuf_tensor("hs_tm", [128, 2, 8], F32))
        ds_tm = EB(nc.sbuf_tensor("ds_tm", [128, 2, 8], F32))
        rt = EB(nc.sbuf_tensor("rt", [128, 10, 8], F32))
        rt1 = EB(nc.sbuf_tensor("rt1", [128, 8], F32))
        rmsk = EB(nc.sbuf_tensor("rmsk", [128, 8], I32))
        neg8 = EB(nc.sbuf_tensor("neg8", [128, 8], F32))
        Ybuf = EB(nc.sbuf_tensor("Ybuf", [128, 4, HID], F32))

        @block.gpsimd
        def _(g):
            g.wait_ge(S.mm.h, qkv_all)
            dmaw(g, S.ldb, nw_sb[:, :], nwcol[:, :])
            dmaw(g, S.ldb, b8_sb[:, :], b8[:, :])
            dmaw(g, S.ldb, A_bf[:, :, :], Amat3[:, :, :])
            dmaw(g, S.ldb, W8_bf[:, :, :], WT[0][:, :, 0:8])
            dmaw(g, S.ldb, BmB[:, :], Bm[:, :], max_dma_last_dim=8192)
            dmaw(g, S.ldb, lim_bf[:, :], limes[:, :])
            dmaw(g, S.ldb, limT_bf[:, :], limesT[:, :])
        b_loads = S.ldb.n

        @block.vector
        def _(v):
            v.memset(neg8[:, :], -1e30)
            v.drain()

        bank_guard = [None, None]
        wbuf_guard = [None, None]
        fslot_guard = {}

        def wait_bank(t, bank):
            if bank_guard[bank] is not None:
                sem, val = bank_guard[bank]
                t.wait_ge(sem.h, val)

        def wait_wbuf(g, wb):
            if wbuf_guard[wb] is not None:
                sem, val = wbuf_guard[wb]
                g.wait_ge(sem.h, val)

        def rsqrt(s, dst, src, scale, bias):
            s.drain()
            s.activation(dst, src, AF.Ln, scale=scale, bias=bias)
            s.drain()
            i = s.activation(dst, dst, AF.Exp, scale=-0.5)
            s.drain()
            return i

        fgrp_state = [0]

        def emit_f32_transposes(src_fn, dst_fn, nw_scale, extra_first_waits=(),
                                per_tt_waits=None):
            first = True
            tiles = [(tt, fc) for tt in range(4) for fc in range(CH)]
            tt_marks = [0, 0, 0, 0]
            for g0 in range(0, len(tiles), 4):
                grp = tiles[g0:g0 + 4]
                bank = (g0 // 4) % 2
                psb = px[bank]
                ttw = None
                if per_tt_waits is not None and grp[0][1] == 0:
                    ttw = per_tt_waits[grp[0][0]]

                @block.tensor
                def _(t, grp=grp, psb=psb, first=first, ttw=ttw,
                      guard=fslot_guard.get(bank), xw=tuple(extra_first_waits)):
                    if first:
                        for sem, val in xw:
                            t.wait_ge(sem.h, val)
                    if ttw is not None:
                        t.wait_ge(ttw[0].h, ttw[1])
                    if guard is not None:
                        t.wait_ge(guard[0].h, guard[1])
                    for k, (tt, fc) in enumerate(grp):
                        i = t.transpose(psb[:, k * 128:(k + 1) * 128],
                                        src_fn(tt, fc), idf[:, :])
                    S.bt.inc(i, 1)

                if bank == 0:
                    @block.vector
                    def _(v, grp=grp, psb=psb, nw=nw_scale):
                        v.wait_ge(S.bt.h, S.bt.n)
                        for k, (tt, fc) in enumerate(grp):
                            if nw is None:
                                i = v.tensor_copy(dst_fn(tt, fc),
                                                  psb[:, k * 128:(k + 1) * 128])
                            else:
                                i = v.tensor_scalar(dst_fn(tt, fc),
                                                    psb[:, k * 128:(k + 1) * 128],
                                                    nw[:, fc:fc + 1], None, ALU.mult)
                        S.bte.inc(i, 1)
                    fslot_guard[bank] = (S.bte, S.bte.n)
                else:
                    @block.scalar
                    def _(s, grp=grp, psb=psb, nw=nw_scale):
                        s.wait_ge(S.bt.h, S.bt.n)
                        for k, (tt, fc) in enumerate(grp):
                            if nw is None:
                                i = s.activation(dst_fn(tt, fc),
                                                 psb[:, k * 128:(k + 1) * 128], AF.Copy)
                            else:
                                i = s.activation(dst_fn(tt, fc),
                                                 psb[:, k * 128:(k + 1) * 128], AF.Copy,
                                                 scale=nw[:, fc:fc + 1])
                        S.bts.inc(i, 1)
                    fslot_guard[bank] = (S.bts, S.bts.n)
                tt_marks[grp[-1][0]] = ((S.bte, S.bte.n), (S.bts, S.bts.n))
                first = False
            return tt_marks

        for cb in range(2):
            @block.gpsimd
            def _(g, cb=cb):
                if cb == 1:
                    g.wait_ge(S.bmm.h, S.bmm.n)
                else:
                    g.wait_ge(S.mm.h, qkv_all)
                dmaw(g, S.ldb, xx_sb[:, :, :], xTl3[:, :, cb * 512:(cb + 1) * 512], max_dma_last_dim=2048)
            xg_done = S.ldb.n

            for tt in range(4):
                @block.sync
                def _(sy, cb=cb, tt=tt):
                    sy.wait_ge(S.cc.h, 2)
                    if cb == 1:
                        sy.wait_ge(S.sq.h, S.sq.n)
                        sy.wait_ge(S.bte.h, S.bte.n)
                        sy.wait_ge(S.bts.h, S.bts.n)
                        sy.wait_ge(S.bmm.h, S.bmm.n)
                        sy.wait_ge(S.nrm.h, S.nrm.n)
                    atv4 = at_v(tt).rearrange("p (sh h f) -> p sh h f", sh=4, h=2)
                    for hlf in range(2):
                        dmaw(sy, S.lat, atv4[:, :, hlf, :],
                             a2a_r[hlf][tt * 128:(tt + 1) * 128, 4 * cb:4 * cb + 4, :])

            nrm_tt = [None] * 4
            x2_tt = [None] * 4
            for cg in range(16 if bstop >= 1 else 0):
                wb = cg % 2

                @block.gpsimd
                def _(g, cb=cb, cg=cg, wb=wb):
                    wait_wbuf(g, wb)
                    if cb == 0 and cg < 2:
                        g.wait_ge(S.mm.h, qkv_all)
                    dmaw(g, S.ldw, wbuf[wb][:, :, :], gatewT[cg], max_dma_last_dim=8192)

                for tt in range(4):
                    gi = cg * 4 + tt
                    bank = gi % 2
                    ps = PSQ[bank]

                    @block.tensor
                    def _(t, cg=cg, tt=tt, wb=wb, ps=ps, bank=bank):
                        t.wait_ge(S.ldw.h, S.ldw.n)
                        if cg == 0 and tt == 0:
                            t.wait_ge(S.ldb.h, xg_done)
                        wait_bank(t, bank)
                        for ch in range(CH):
                            i = t.matmul(ps[:, 0:256],
                                         xx_sb[:, ch, tt * 128:(tt + 1) * 128],
                                         wbuf[wb][:, ch, :],
                                         start=(ch == 0), stop=(ch == CH - 1))
                        S.bmm.inc(i, 1)

                    @block.scalar
                    def _(s, cb=cb, cg=cg, tt=tt, ps=ps):
                        s.wait_ge(S.bmm.h, S.bmm.n)
                        if cb == 0 and cg == 0 and tt == 0:
                            s.wait_ge(S.ast.h, phase_a_ast)
                        if cb == 1 and cg == 0 and tt == 0 and bstop >= 6:
                            s.wait_ge(S.stx.h, 64)
                        i = s.activation(Ybuf[:, tt, cg * 256:(cg + 1) * 256],
                                         ps[:, 0:256], AF.Sigmoid)
                        S.sig.inc(i, 1)
                    bank_guard[bank] = (S.sig, S.sig.n)
                wbuf_guard[wb] = (S.bmm, S.bmm.n)

            for tt in range(4):
                @block.scalar
                def _(s, cb=cb, tt=tt):
                    s.wait_ge(S.lat.h, 32 * (cb * 4 + tt + 1))
                    if cb == 0 and tt == 0:
                        s.wait_ge(S.ldb.h, b_loads)
                    s.activation(u1[:, :], at_v(tt), AF.Square,
                                 accum_out=stats[:, tt:tt + 1])
                    i = rsqrt(s, stats[:, 4 + tt:5 + tt], stats[:, tt:tt + 1],
                              1.0 / HID, EPS_ATTN)
                    S.ln.inc(i, 1)

                @block.vector
                def _(v, tt=tt):
                    v.wait_ge(S.ln.h, S.ln.n)
                    v.wait_ge(S.sig.h, S.sig.n)
                    i = v.scalar_tensor_tensor(Ybuf[:, tt, :], at_v(tt),
                                               stats[:, 4 + tt:5 + tt], Ybuf[:, tt, :],
                                               op0=ALU.mult, op1=ALU.mult)
                    S.nrm.inc(i, 1)
                nrm_tt[tt] = (S.nrm, S.nrm.n)

            gnT_tt = [None] * 4
            if bstop >= 2:
                gnT_tt = emit_f32_transposes(
                    src_fn=lambda tt, fc: Ybuf[:, tt, fc * 128:(fc + 1) * 128],
                    dst_fn=lambda tt, fc: gnT_v(fc, tt * 128, (tt + 1) * 128),
                    nw_scale=nw_sb,
                    per_tt_waits=nrm_tt)
            gnT_done = S.bte.n

            for cg in range(16 if bstop >= 3 else 0):
                wb = cg % 2

                @block.gpsimd
                def _(g, cg=cg, wb=wb):
                    wait_wbuf(g, wb)
                    dmaw(g, S.ldw, wbuf[wb][:, :, :], outwT[cg], max_dma_last_dim=8192)

                for tt in range(4):
                    gi = cg * 4 + tt
                    bank = gi % 2
                    ps = PSQ[bank]

                    @block.tensor
                    def _(t, cg=cg, tt=tt, wb=wb, ps=ps, bank=bank):
                        t.wait_ge(S.ldw.h, S.ldw.n)
                        if cg == 0 and gnT_tt[tt] is not None:
                            for sem, val in gnT_tt[tt]:
                                t.wait_ge(sem.h, val)
                        wait_bank(t, bank)
                        for ch in range(CH):
                            i = t.matmul(ps[:, 0:256],
                                         gnT_v(ch, tt * 128, (tt + 1) * 128),
                                         wbuf[wb][:, ch, :],
                                         start=(ch == 0), stop=(ch == CH - 1))
                        S.bmm.inc(i, 1)

                    @block.vector
                    def _(v, cg=cg, tt=tt, ps=ps):
                        v.wait_ge(S.bmm.h, S.bmm.n)
                        i = v.tensor_copy(Ybuf[:, tt, cg * 256:(cg + 1) * 256], ps[:, 0:256])
                        S.ye.inc(i, 1)
                    bank_guard[bank] = (S.ye, S.ye.n)
                wbuf_guard[wb] = (S.bmm, S.bmm.n)
            oproj_done = S.ye.n

            for tt in range(4 if bstop >= 4 else 0):
                @block.sync
                def _(sy, cb=cb, tt=tt):
                    sy.wait_ge(S.ye.h, S.ye.n)
                    dmaw(sy, S.xr, u1[:, :], x_rows[cb * 512 + tt * 128:cb * 512 + (tt + 1) * 128, :])

                @block.vector
                def _(v, tt=tt):
                    v.wait_ge(S.xr.h, S.xr.n)
                    v.drain()
                    i = v.scalar_tensor_tensor(Ybuf[:, tt, :], Ybuf[:, tt, :], 1.0,
                                               u1[:, :], op0=ALU.mult, op1=ALU.add,
                                               accum_out=stats[:, 8 + tt:9 + tt])
                    S.ye.inc(i, 1)

                @block.scalar
                def _(s, tt=tt):
                    s.wait_ge(S.ye.h, S.ye.n)
                    i = s.activation(at_v(tt), Ybuf[:, tt, :], AF.Square,
                                     accum_out=stats[:, 12 + tt:13 + tt])
                    S.sq.inc(i, 1)

                @block.vector
                def _(v, tt=tt):
                    v.wait_ge(S.sq.h, S.sq.n)
                    v.drain()
                    v.tensor_scalar(stats[:, 16 + tt:17 + tt], stats[:, 8 + tt:9 + tt],
                                    1.0 / HID, None, ALU.mult)
                    v.tensor_scalar(stats[:, 20 + tt:21 + tt], stats[:, 12 + tt:13 + tt],
                                    1.0 / HID, None, ALU.mult)
                    v.drain()
                    v.tensor_tensor(stats[:, 24 + tt:25 + tt],
                                    stats[:, 16 + tt:17 + tt],
                                    stats[:, 16 + tt:17 + tt], ALU.mult)
                    v.drain()
                    i = v.tensor_tensor(stats[:, 20 + tt:21 + tt],
                                        stats[:, 20 + tt:21 + tt],
                                        stats[:, 24 + tt:25 + tt], ALU.subtract)
                    S.misc.inc(i, 1)

                @block.scalar
                def _(s, tt=tt):
                    s.wait_ge(S.misc.h, S.misc.n)
                    i = rsqrt(s, stats[:, 24 + tt:25 + tt], stats[:, 20 + tt:21 + tt],
                              1.0, EPS_LN)
                    S.ln.inc(i, 1)

                @block.vector
                def _(v, tt=tt):
                    v.wait_ge(S.ln.h, S.ln.n)
                    v.drain()
                    i = v.tensor_scalar(Ybuf[:, tt, :], Ybuf[:, tt, :],
                                        stats[:, 16 + tt:17 + tt],
                                        stats[:, 24 + tt:25 + tt],
                                        ALU.subtract, op1=ALU.mult)
                    S.ye.inc(i, 1)
                x2_tt[tt] = (S.ye, S.ye.n)
            x2_done = S.ye.n

            if bstop >= 5:
                emit_f32_transposes(
                    src_fn=lambda tt, fc: Ybuf[:, tt, fc * 128:(fc + 1) * 128],
                    dst_fn=lambda tt, fc: xx_sb[:, fc, tt * 128:(tt + 1) * 128],
                    nw_scale=None,
                    extra_first_waits=[(S.bmm, S.bmm.n)],
                    per_tt_waits=x2_tt)
            x2T_done = (S.bte.n, S.bts.n)

            if bstop < 6:
                continue
            @block.tensor
            def _(t, cb=cb):
                t.wait_ge(S.bte.h, x2T_done[0])
                t.wait_ge(S.bts.h, x2T_done[1])
                t.wait_ge(S.misc.h, S.misc.n)
                t.wait_ge(S.ldb.h, b_loads)
                for ch in range(CH):
                    i = t.matmul(pn[0:32, 0:512], A_bf[:, ch, :],
                                 xx_sb[:, ch, :],
                                 start=(ch == 0), stop=(ch == CH - 1))
                S.bmm.inc(i, 1)

            @block.vector
            def _(v, cb=cb):
                v.wait_ge(S.bmm.h, S.bmm.n)
                i = v.tensor_copy(zT_sb[cb][:, :], pn[0:32, 0:512])
                S.misc.inc(i, 1)
            zt_done = S.misc.n

            @block.tensor
            def _(t, cb=cb):
                t.wait_ge(S.misc.h, zt_done)
                for ch in range(CH):
                    i = t.matmul(pn[0:8, 0:512], W8_bf[:, ch, :],
                                 xx_sb[:, ch, :],
                                 start=(ch == 0), stop=(ch == CH - 1))
                S.bmm.inc(i, 1)

            @block.vector
            def _(v, cb=cb):
                v.wait_ge(S.bmm.h, S.bmm.n)
                v.tensor_scalar(hsA[cb][:, :], pn[0:8, 1:512:2], b8_sb[:, :],
                                None, ALU.add)
                v.drain()
                i = v.tensor_reduce(mx8[:, cb:cb + 1], hsA[cb][:, :], axis=AX,
                                    op=ALU.max, apply_absolute_value=True)
                S.misc.inc(i, 1)
            hs_done = S.misc.n

            @block.tensor
            def _(t, cb=cb):
                t.wait_ge(S.misc.h, hs_done)
                i = t.matmul(pn[0:8, 0:512], limT_bf[:, :], zT_sb[cb][:, :],
                             start=True, stop=True)
                S.bmm.inc(i, 1)

            @block.vector
            def _(v, cb=cb):
                v.wait_ge(S.bmm.h, S.bmm.n)
                v.tensor_copy(dsA[cb][:, :], pn[0:8, 1:512:2])
                v.drain()
                i = v.tensor_reduce(mx8[:, 2 + cb:3 + cb], dsA[cb][:, :], axis=AX,
                                    op=ALU.max, apply_absolute_value=True)
                S.misc.inc(i, 1)

            if cb == 0:
                for tt in range(4):
                    @block.sync
                    def _(sy, tt=tt):
                        sy.wait_ge(S.bte.h, x2T_done[0])
                        sy.wait_ge(S.bts.h, x2T_done[1])
                        dmaw(sy, S.stx, x2stash.ap()[tt * 128:(tt + 1) * 128, :], Ybuf[:, tt, :])

        router_done = [0]
        b_row = big1[0:1, 0:HID]
        if bstop >= 9:
            @block.gpsimd
            def _(g):
                g.wait_ge(S.sq.h, S.sq.n)
                g.wait_ge(S.nrm.h, S.nrm.n)
                dmaw(g, S.ldb, b_row, Bm[32:33, :], max_dma_last_dim=8192)
        b_row_ld = S.ldb.n

        def b2_pass1(cb, x2T_ready):
            for cg in range(16):
                wb = cg % 2

                @block.gpsimd
                def _(g, cg=cg, wb=wb):
                    wait_wbuf(g, wb)
                    dmaw(g, S.ldw, wbuf[wb][:, :, :], WT[cg], max_dma_last_dim=8192)

                for tt in range(4):
                    bank = (cg * 4 + tt) % 2
                    ps = PSQ[bank]

                    @block.tensor
                    def _(t, cb=cb, cg=cg, tt=tt, wb=wb, ps=ps, bank=bank,
                          x2T_ready=x2T_ready):
                        t.wait_ge(S.ldw.h, S.ldw.n)
                        if cg == 0 and tt == 0:
                            t.wait_ge(S.bte.h, x2T_ready[0])
                            t.wait_ge(S.bts.h, x2T_ready[1])
                            t.wait_ge(S.ldb.h, b_row_ld)
                        wait_bank(t, bank)
                        t.matmul(ps[:, 0:256], ones_bf[0:1, :],
                                 b_row[:, cg * 256:(cg + 1) * 256],
                                 start=True, stop=False)
                        for ch in range(CH):
                            i = t.matmul(ps[:, 0:256],
                                         xx_sb[:, ch, tt * 128:(tt + 1) * 128],
                                         wbuf[wb][:, ch, :],
                                         start=False, stop=(ch == CH - 1))
                        S.bmm.inc(i, 1)

                    @block.vector
                    def _(v, cg=cg, tt=tt, ps=ps):
                        v.wait_ge(S.bmm.h, S.bmm.n)
                        i = v.scalar_tensor_tensor(
                            Ybuf[:, tt, cg * 256:(cg + 1) * 256], ps[:, 0:256], 1.0,
                            Ybuf[:, tt, cg * 256:(cg + 1) * 256],
                            op0=ALU.mult, op1=ALU.add)
                        S.ye.inc(i, 1)
                    bank_guard[bank] = (S.ye, S.ye.n)
                wbuf_guard[wb] = (S.bmm, S.bmm.n)

        def b2_pass2(cb):
            for cg in range(16):
                for tt in range(4):
                    bank = (cg * 4 + tt) % 2
                    ps = PSQ[bank]

                    @block.tensor
                    def _(t, cb=cb, cg=cg, tt=tt, ps=ps, bank=bank,
                          first=(cg == 0 and tt == 0)):
                        if first:
                            t.wait_ge(S.misc.h, router_done[0])
                        wait_bank(t, bank)
                        i = t.matmul(ps[:, 0:256],
                                     zpT_sb[cb][:, tt * 128:(tt + 1) * 128],
                                     BmB[0:32, cg * 256:(cg + 1) * 256],
                                     start=True, stop=True)
                        S.bmm.inc(i, 1)

                    @block.vector
                    def _(v, cg=cg, tt=tt, ps=ps):
                        v.wait_ge(S.bmm.h, S.bmm.n)
                        i = v.scalar_tensor_tensor(
                            Ybuf[:, tt, cg * 256:(cg + 1) * 256], ps[:, 0:256], 1.0,
                            Ybuf[:, tt, cg * 256:(cg + 1) * 256],
                            op0=ALU.mult, op1=ALU.add,
                            accum_out=s2buf[:, tt * 16 + cg:tt * 16 + cg + 1])
                        S.ye.inc(i, 1)
                    bank_guard[bank] = (S.ye, S.ye.n)

                    @block.scalar
                    def _(s, cg=cg, tt=tt):
                        s.wait_ge(S.ye.h, S.ye.n)
                        s.drain()
                        i = s.activation(dump[:, :], Ybuf[:, tt, cg * 256:(cg + 1) * 256],
                                         AF.Square,
                                         accum_out=q2buf[:, tt * 16 + cg:tt * 16 + cg + 1])
                        S.sq.inc(i, 1)

        def b2_ln2out(cb):
            for tt in range(4):
                @block.vector
                def _(v, tt=tt):
                    v.wait_ge(S.sq.h, S.sq.n)
                    v.drain()
                    v.tensor_reduce(stats[:, 8 + tt:9 + tt], s2buf[:, tt * 16:(tt + 1) * 16],
                                    axis=AX, op=ALU.add)
                    v.tensor_reduce(stats[:, 12 + tt:13 + tt], q2buf[:, tt * 16:(tt + 1) * 16],
                                    axis=AX, op=ALU.add)
                    v.drain()
                    v.tensor_scalar(stats[:, 16 + tt:17 + tt], stats[:, 8 + tt:9 + tt],
                                    1.0 / HID, None, ALU.mult)
                    v.tensor_scalar(stats[:, 20 + tt:21 + tt], stats[:, 12 + tt:13 + tt],
                                    1.0 / HID, None, ALU.mult)
                    v.drain()
                    v.tensor_tensor(stats[:, 24 + tt:25 + tt],
                                    stats[:, 16 + tt:17 + tt],
                                    stats[:, 16 + tt:17 + tt], ALU.mult)
                    v.drain()
                    i = v.tensor_tensor(stats[:, 20 + tt:21 + tt],
                                        stats[:, 20 + tt:21 + tt],
                                        stats[:, 24 + tt:25 + tt], ALU.subtract)
                    S.misc.inc(i, 1)

                @block.scalar
                def _(s, tt=tt):
                    s.wait_ge(S.misc.h, S.misc.n)
                    i = rsqrt(s, stats[:, 24 + tt:25 + tt], stats[:, 20 + tt:21 + tt],
                              1.0, EPS_LN)
                    S.ln.inc(i, 1)

                @block.vector
                def _(v, tt=tt):
                    v.wait_ge(S.ln.h, S.ln.n)
                    v.drain()
                    i = v.tensor_scalar(Ybuf[:, tt, :], Ybuf[:, tt, :],
                                        stats[:, 16 + tt:17 + tt],
                                        stats[:, 24 + tt:25 + tt],
                                        ALU.subtract, op1=ALU.mult)
                    S.misc.inc(i, 1)

                @block.sync
                def _(sy, cb=cb, tt=tt):
                    sy.wait_ge(S.misc.h, S.misc.n)
                    dmaw(sy, S.out, out_e[cb * 512 + tt * 128:cb * 512 + (tt + 1) * 128, :],
                         Ybuf[:, tt, :])

        if bstop < 7:
            pb.close()
            pb_psum.close()
            return nc
        @block.vector
        def _(v):
            v.wait_ge(S.misc.h, S.misc.n)
            v.drain()
            v.tensor_tensor(mx2[:, 0:1], mx8[:, 0:1], mx8[:, 1:2], ALU.max)
            v.drain()
            i = v.tensor_tensor(mx2[:, 1:2], mx8[:, 2:3], mx8[:, 3:4], ALU.max)
            S.misc.inc(i, 1)

        @block.tensor
        def _(t):
            t.wait_ge(S.misc.h, S.misc.n)
            i = t.transpose(px[0][0:2, 0:8], mx2[0:8, 0:2], idf[0:8, 0:8])
            S.bt.inc(i, 1)

        @block.vector
        def _(v):
            v.wait_ge(S.bt.h, S.bt.n)
            v.tensor_copy(arv[:, :], px[0][0:2, 0:8])
            v.drain()
            i = v.tensor_reduce(ar2[:, :], arv[:, :], axis=AX, op=ALU.max)
            S.misc.inc(i, 1)

        @block.sync
        def _(sy):
            sy.wait_ge(S.misc.h, S.misc.n)
            sy.wait_ge(S.arw.h, ar_zero_done)
            dmaw(sy, S.arw, ar_in.ap()[0:1, 0:2], ar2[0:2, 0:1])

        @block.gpsimd
        def _(g):
            g.wait_ge(S.arw.h, S.arw.n)
            i = g.collective_compute(
                "AllReduce", ALU.max,
                replica_groups=GROUPS_ALL,
                ins=[ar_in.ap().opt()],
                outs=[ar_out.ap().opt()],
            )
            S.cc.inc(i, 1)
        ar_cc = S.cc.n

        @block.sync
        def _(sy):
            sy.wait_ge(S.cc.h, ar_cc)
            dmaw(sy, S.arw, arr[:, :], ar_out.ap()[:, :])

        @block.vector
        def _(v):
            v.wait_ge(S.arw.h, S.arw.n)
            v.tensor_scalar(inv2[:, :], arr[0:1, 0:2], 1e-6, None, ALU.max)
            v.drain()
            v.reciprocal(inv2[:, :], inv2[:, :])
            v.drain()
            i = v.tensor_scalar(inv2[:, :], inv2[:, :], 0.5, None, ALU.mult)
            S.misc.inc(i, 1)


        if bstop >= 9:
            b2_pass1(1, x2T_ready=(S.bte.n, S.bts.n))
        @block.tensor
        def _(t):
            t.wait_ge(S.misc.h, S.misc.n)
            i = t.matmul(pn[:, 256:258], onesf[:, :], inv2[:, :], start=True, stop=True)
            S.bt.inc(i, 1)

        @block.vector
        def _(v):
            v.wait_ge(S.bt.h, S.bt.n)
            i = v.tensor_copy(invb[:, :], pn[:, 256:258])
            v.drain()
            S.misc.inc(i, 1)
        invb_done = S.misc.n

        for cb in range(2 if bstop >= 8 else 0):
            for at in range(2):
                @block.tensor
                def _(t, cb=cb, at=at):
                    t.wait_ge(S.misc.h, S.misc.n)
                    t.transpose(px[0][0:128, 0:8], hsA[cb][:, at * 128:(at + 1) * 128],
                                idf[0:8, 0:8])
                    i = t.transpose(px[0][0:128, 8:16], dsA[cb][:, at * 128:(at + 1) * 128],
                                    idf[0:8, 0:8])
                    S.bt.inc(i, 1)

                @block.vector
                def _(v, at=at):
                    v.wait_ge(S.bt.h, S.bt.n)
                    v.tensor_copy(hs_tm[:, at, :], px[0][0:128, 0:8])
                    v.tensor_copy(ds_tm[:, at, :], px[0][0:128, 8:16])
                    v.drain()
                    v.tensor_scalar(rt[:, 0, :], hs_tm[:, at, :], invb[:, 0:1], None, ALU.mult)
                    v.drain()
                    i = v.scalar_tensor_tensor(rt[:, 1, :], ds_tm[:, at, :], invb[:, 1:2],
                                               rt[:, 0, :], op0=ALU.mult, op1=ALU.add)
                    S.misc.inc(i, 1)

                @block.scalar
                def _(s):
                    s.wait_ge(S.misc.h, S.misc.n)
                    i = s.activation(rt[:, 2, :], rt[:, 1, :], AF.Exp)
                    S.ln.inc(i, 1)

                @block.vector
                def _(v):
                    v.wait_ge(S.ln.h, S.ln.n)
                    v.tensor_reduce(rt1[:, 0:1], rt[:, 2, :], axis=AX, op=ALU.add)
                    v.drain()
                    v.reciprocal(rt1[:, 1:2], rt1[:, 0:1])
                    v.drain()
                    v.tensor_scalar(rt[:, 3, :], rt[:, 2, :], rt1[:, 1:2], None, ALU.mult)
                    v.drain()
                    v.tensor_reduce(rt1[:, 2:3], rt[:, 3, :], axis=AX, op=ALU.max)
                    v.drain()
                    v.tensor_scalar(rmsk[:, :], rt[:, 3, :], rt1[:, 2:3], None, ALU.is_lt)
                    v.drain()
                    v.tensor_copy(rt[:, 4, :], neg8[:, :])
                    v.drain()
                    v.copy_predicated(rt[:, 4, :], rmsk[:, :], rt[:, 3, :])
                    v.drain()
                    v.tensor_reduce(rt1[:, 3:4], rt[:, 4, :], axis=AX, op=ALU.max)
                    v.tensor_scalar(rt[:, 5, :], rt[:, 3, :], rt1[:, 2:3], None, ALU.is_ge)
                    v.drain()
                    v.tensor_reduce(rt1[:, 4:5], rt[:, 5, :], axis=AX, op=ALU.add)
                    v.drain()
                    v.tensor_scalar(rmsk[:, 0:1], rt1[:, 4:5], 1.5, None, ALU.is_gt)
                    v.drain()
                    v.tensor_copy(rt1[:, 5:6], rt1[:, 3:4])
                    v.drain()
                    v.copy_predicated(rt1[:, 5:6], rmsk[:, 0:1], rt1[:, 2:3])
                    v.drain()
                    i = v.tensor_scalar(rt1[:, 6:7], rt1[:, 5:6], 2.0, None, ALU.mult)
                    S.misc.inc(i, 1)

                @block.scalar
                def _(s):
                    s.wait_ge(S.misc.h, S.misc.n)
                    i = s.activation(rt[:, 6, :], rt[:, 3, :], AF.Exp, scale=-2.0,
                                     bias=rt1[:, 6:7])
                    S.ln.inc(i, 1)

                @block.vector
                def _(v):
                    v.wait_ge(S.ln.h, S.ln.n)
                    v.tensor_scalar(rt[:, 7, :], rt[:, 6, :], 1.0, None, ALU.add)
                    v.drain()
                    v.reciprocal(rt[:, 8, :], rt[:, 7, :])
                    v.drain()
                    v.tensor_tensor(rt[:, 9, :], rt[:, 3, :], rt[:, 8, :], ALU.mult)
                    v.drain()
                    v.tensor_reduce(rt1[:, 7:8], rt[:, 9, :], axis=AX, op=ALU.add)
                    v.drain()
                    v.tensor_scalar(rt1[:, 4:5], rt1[:, 7:8], 1e-9, None, ALU.add)
                    v.drain()
                    v.reciprocal(rt1[:, 5:6], rt1[:, 4:5])
                    v.drain()
                    i = v.tensor_scalar(rt[:, 0, :], rt[:, 9, :], rt1[:, 5:6], None, ALU.mult)
                    v.drain()
                    S.misc.inc(i, 1)

                @block.tensor
                def _(t):
                    t.wait_ge(S.misc.h, S.misc.n)
                    i = t.transpose(px[0][0:8, 0:128], rt[:, 0, :], idf[:, :])
                    S.bt.inc(i, 1)

                @block.vector
                def _(v, cb=cb, at=at):
                    v.wait_ge(S.bt.h, S.bt.n)
                    v.tensor_copy(wfT[cb][:, at * 256 + 0:at * 256 + 256:2], px[0][0:8, 0:128])
                    i = v.tensor_copy(wfT[cb][:, at * 256 + 1:at * 256 + 256:2], px[0][0:8, 0:128])
                    S.misc.inc(i, 1)
            wf_done = S.misc.n

            @block.tensor
            def _(t, cb=cb):
                t.wait_ge(S.misc.h, wf_done)
                i = t.matmul(pn[0:32, 0:512], lim_bf[:, :], wfT[cb][:, :],
                             start=True, stop=True)
                S.bmm.inc(i, 1)

            @block.vector
            def _(v, cb=cb):
                v.wait_ge(S.bmm.h, S.bmm.n)
                v.drain()
                i = v.scalar_tensor_tensor(zpT_sb[cb][:, :], zT_sb[cb][:, :], 0.5,
                                           pn[0:32, 0:512], op0=ALU.mult, op1=ALU.mult)
                S.misc.inc(i, 1)
        router_done[0] = S.misc.n

        if bstop >= 9:
            b2_pass2(1)
            b2_ln2out(1)

            rld_tt = [None] * 4
            for tt in range(4):
                @block.sync
                def _(sy, tt=tt):
                    sy.wait_ge(S.out.h, 16 * (tt + 1))
                    dmaw(sy, S.stx, Ybuf[:, tt, :], x2stash.ap()[tt * 128:(tt + 1) * 128, :])
                rld_tt[tt] = (S.stx, S.stx.n)
            stx_reload = S.stx.n
            emit_f32_transposes(
                src_fn=lambda tt, fc: Ybuf[:, tt, fc * 128:(fc + 1) * 128],
                dst_fn=lambda tt, fc: xx_sb[:, fc, tt * 128:(tt + 1) * 128],
                nw_scale=None,
                extra_first_waits=[(S.bmm, S.bmm.n), (S.ye, S.ye.n)],
                per_tt_waits=rld_tt)
            b2_pass1(0, x2T_ready=(S.bte.n, S.bts.n))
            b2_pass2(0)
            b2_ln2out(0)

        @block.sync
        def _(sy):
            sy.wait_ge(S.out.h, S.out.n)

        @block.gpsimd
        def _(g):
            g.wait_ge(S.out.h, S.out.n)

        pb.close()
        pb_psum.close()

    return nc


def _host_prep(inputs):
    x = np.asarray(inputs["L0_attn_hidden_states"], dtype=np.float32)
    qkv_w = np.asarray(inputs["L0_attn_qkv_w"], dtype=np.float32)
    out_w = np.asarray(inputs["L0_attn_out_w"], dtype=np.float32)
    gate_w = np.asarray(inputs["L0_attn_gate_w"], dtype=np.float32)
    norm_w = np.asarray(inputs["L0_attn_norm_w"], dtype=np.float32)
    W = np.asarray(inputs["L0_ffn_W"], dtype=np.float32)
    b = np.asarray(inputs["L0_ffn_b"], dtype=np.float32)
    A = np.asarray(inputs["L0_ffn_A"], dtype=np.float32)
    Bmat = np.asarray(inputs["L0_ffn_Bmat"], dtype=np.float32)
    LiMEs = np.asarray(inputs["L0_ffn_LiMEs"], dtype=np.float32)

    xf = np.ascontiguousarray(x.reshape(8192, HID))
    xT = [np.ascontiguousarray(xf[bb * 4096:(bb + 1) * 4096].T) for bb in range(2)]
    qkvT = np.ascontiguousarray(qkv_w.T)

    def tile_w(wT):
        return np.ascontiguousarray(
            wT.reshape(32, 128, 16, 256).transpose(2, 0, 1, 3)
              .reshape(16, 32, 128, 256).transpose(0, 2, 1, 3))

    gateT = tile_w(np.ascontiguousarray(gate_w.T))
    outT = tile_w(np.ascontiguousarray(out_w.T))
    WTm_flat = np.ascontiguousarray(W.T)
    WTm = tile_w(WTm_flat)
    nwcol_h = np.ascontiguousarray(norm_w.reshape(32, 128).T)
    BmB_h = np.ascontiguousarray(np.concatenate([Bmat, b[None, :]], axis=0))
    b8v = np.ascontiguousarray(b[:8, None])
    limesT = np.ascontiguousarray(LiMEs.T)

    HEADS = 32
    base = 1.0 / 2.0 ** (8.0 / HEADS)
    factor = 1.0 - 1.0 / (31 + 1e-5) + 1e-5
    slope = np.array([base ** (h + 1) * factor for h in range(HEADS)], dtype=np.float64)
    r = np.arange(256, dtype=np.float64) + 1.0
    in_maps = []
    for c in range(NCORES):
        bb = c // 4
        heads = [8 * (c % 4) + i for i in range(8)]
        sl = slope[heads]
        qd = np.exp(-sl[:, None] * r[None, :])
        kd = np.exp(-sl[:, None] * (256.0 - r)[None, :])
        bd = np.exp(-sl * 256.0)
        diff = r[:, None] - r[None, :]
        dg = np.exp(-np.minimum(sl[:, None, None] * diff[None, :, :], 745.0))
        dg = np.where(diff[None, :, :] >= 0, dg, 0.0)
        dgT = dg.transpose(0, 2, 1)
        diag_in = np.ascontiguousarray(
            dgT.reshape(8, 2, 128, 256).transpose(2, 0, 1, 3).astype(np.float32))
        qdec_in = np.ascontiguousarray(
            qd.reshape(8, 2, 128).transpose(2, 0, 1).reshape(128, 16).astype(np.float32))
        kdec_in = np.ascontiguousarray(
            kd.reshape(8, 2, 128).transpose(2, 0, 1).reshape(128, 16).astype(np.float32))
        bdec_in = np.ascontiguousarray(
            np.broadcast_to(bd[None, :], (128, 8)).astype(np.float32))
        s0 = 512 * c
        in_maps.append({
            "xT_b": xT[bb],
            "xTloc": np.ascontiguousarray(
                np.concatenate([xT[0][:, s0:s0 + 512], xT[1][:, s0:s0 + 512]], axis=1)),
            "x_rows": np.ascontiguousarray(
                np.concatenate([xf[s0:s0 + 512], xf[4096 + s0:4096 + s0 + 512]], axis=0)),
            "qkvwT": np.ascontiguousarray(qkvT[:, 3072 * (c % 4):3072 * (c % 4) + 3072]),
            "gatewT": gateT,
            "outwT": outT,
            "WT": WTm,
            "Amat": A,
            "Bm": BmB_h,
            "limes": LiMEs,
            "limesT": limesT,
            "nwcol": nwcol_h,
            "b8": b8v,
            "diag": diag_in,
            "qdec": qdec_in,
            "kdec": kdec_in,
            "bdec": bdec_in,
        })
    return in_maps


def kernel(**inputs):
    in_maps = _host_prep(inputs)
    nc = build_nc(stage="full")
    res = run_bass_kernel_spmd(nc, in_maps, core_ids=list(range(NCORES)))
    full = np.empty((2, 4096, HID), dtype=np.float32)
    for c in range(NCORES):
        o = np.asarray(res.results[c]["out"])
        full[0, 512 * c:512 * c + 512] = o[0:512]
        full[1, 512 * c:512 * c + 512] = o[512:1024]
    return full


# version 11 (final)
# speedup vs baseline: 1.0550x; 1.0550x over previous
import numpy as np
import ml_dtypes
import concourse.bass as bass
import concourse.mybir as mybir
from concourse.bass_utils import run_bass_kernel_spmd
from concourse.aot_env import allow_ici_all_to_all
from contextlib import ExitStack

F32 = mybir.dt.float32
BF16 = mybir.dt.bfloat16
I32 = mybir.dt.int32
AF = mybir.ActivationFunctionType
ALU = mybir.AluOpType
AX = mybir.AxisListType.X

NCORES = 8
HID = 4096
SEQ = 4096
NH_SP = 4
NBLK = 16
CH = 32
EPS_ATTN = 1e-5
EPS_LN = 1e-5
GROUPS_BATCH = [[0, 1, 2, 3], [4, 5, 6, 7]]
GROUPS_ALL = [list(range(NCORES))]

SEM_NAMES = ("ldx ldq mm ev tp ktd vt sc dk im umm isc ae kv kvp ast cc init xc "
             "lat ldw bmm sig gn bt bte bts ye sq ln misc xr stx out ldb arw nrm").split()


class Sem:
    def __init__(self, ctx, nc, name):
        self.h = ctx.enter_context(nc.semaphore(name))
        self.n = 0

    def inc(self, instr, k=1):
        instr.then_inc(self.h, k)
        self.n += k


class Sems:
    def __init__(self, ctx, nc):
        for nm in SEM_NAMES:
            setattr(self, nm, Sem(ctx, nc, "s_" + nm))


def build_nc(stage="full", nblk=NBLK, dbg_disable=(), bstop=99):
    nc = bass.Bass("TRN2", target_bir_lowering=False, debug=False)

    xT_b = nc.dram_tensor("xT_b", [HID, SEQ], F32, kind="ExternalInput").ap()
    xTloc = nc.dram_tensor("xTloc", [HID, 1024], F32, kind="ExternalInput").ap()
    x_rows = nc.dram_tensor("x_rows", [1024, HID], F32, kind="ExternalInput").ap()
    qkvwT = nc.dram_tensor("qkvwT", [HID, 3072], F32, kind="ExternalInput").ap()
    gatewT = nc.dram_tensor("gatewT", [16, 128, CH, 256], F32, kind="ExternalInput").ap()
    outwT = nc.dram_tensor("outwT", [16, 128, CH, 256], F32, kind="ExternalInput").ap()
    WT = nc.dram_tensor("WT", [16, 128, CH, 256], F32, kind="ExternalInput").ap()
    Amat = nc.dram_tensor("Amat", [HID, 32], F32, kind="ExternalInput").ap()
    Bm = nc.dram_tensor("Bm", [33, HID], F32, kind="ExternalInput").ap()
    limes = nc.dram_tensor("limes", [8, 32], F32, kind="ExternalInput").ap()
    limesT = nc.dram_tensor("limesT", [32, 8], F32, kind="ExternalInput").ap()
    nwcol = nc.dram_tensor("nwcol", [128, 32], F32, kind="ExternalInput").ap()
    b8 = nc.dram_tensor("b8", [8, 1], F32, kind="ExternalInput").ap()
    diag = nc.dram_tensor("diag", [128, 8, 2, 256], F32, kind="ExternalInput").ap()
    qdec = nc.dram_tensor("qdec", [128, 16], F32, kind="ExternalInput").ap()
    kdec = nc.dram_tensor("kdec", [128, 16], F32, kind="ExternalInput").ap()
    bdec = nc.dram_tensor("bdec", [128, 8], F32, kind="ExternalInput").ap()
    out_e = nc.dram_tensor("out", [1024, HID], F32, kind="ExternalOutput").ap()

    a2a_in = [nc.dram_tensor(f"a2a_in{i}", [4096, 512], BF16, kind="Internal") for i in range(2)]
    a2a_out = [nc.dram_tensor(f"a2a_out{i}", [4096, 512], BF16, kind="Internal") for i in range(2)]
    ar_in = nc.dram_tensor("ar_in", [1, 16], F32, kind="Internal")
    ar_out = nc.dram_tensor("ar_out", [1, 16], F32, kind="Internal")
    x2stash = nc.dram_tensor("x2stash", [512, HID], F32, kind="Internal")

    with ExitStack() as top:
        E = top.enter_context
        S = Sems(top, nc)

        def dmaw(eng, sem, out, in_, **kw):
            eng.wait_ge(sem.h, sem.n)
            sem.inc(eng.dma_start(out=out, in_=in_, **kw), 16)

        ps_q0 = E(nc.psum_tensor("ps_q0", [128, 512], F32))
        ps_q1 = E(nc.psum_tensor("ps_q1", [128, 512], F32))
        PSQ = [ps_q0, ps_q1]
        pa_psum = ExitStack()
        ps_sb = [pa_psum.enter_context(nc.psum_tensor(f"ps_s{i}", [128, 512], F32)) for i in range(2)]
        ps_in = [pa_psum.enter_context(nc.psum_tensor(f"ps_in{i}", [128, 512], F32)) for i in range(2)]
        ps_tb = [pa_psum.enter_context(nc.psum_tensor(f"ps_t{i}", [128, 1024], BF16)) for i in range(2)]

        block = E(nc.Block())

        ident_i = E(nc.sbuf_tensor("ident_i", [128, 128], I32))
        idb = E(nc.sbuf_tensor("idb", [128, 128], BF16))
        idf = E(nc.sbuf_tensor("idf", [128, 128], F32))
        onesf = E(nc.sbuf_tensor("onesf", [1, 128], F32))
        ones_bf = E(nc.sbuf_tensor("ones_bf", [33, 128], BF16))
        zero16 = E(nc.sbuf_tensor("zero16", [1, 16], F32))
        cst_eps = E(nc.sbuf_tensor("cst_eps", [128, 1], F32))
        nc.const_aps.register(F32, 1e-05, cst_eps[:, :])

        @block.gpsimd
        def _(g):
            g.iota(ident_i[:, :], pattern=[[1, 128]], base=0, channel_multiplier=-1)
            g.memset(onesf[:, :], 1.0)
            g.memset(ones_bf[:, :], 1.0)
            g.memset(zero16[:, :], 0.0)
            i = g.memset(cst_eps[:, :], 1e-05)
            S.init.inc(i, 1)

        @block.vector
        def _(v):
            v.wait_ge(S.init.h, 1)
            v.tensor_scalar(idb[:, :], ident_i[:, :], 0.0, None, ALU.is_equal)
            i = v.tensor_scalar(idf[:, :], ident_i[:, :], 0.0, None, ALU.is_equal)
            S.misc.inc(i, 1)

        @block.sync
        def _(sy):
            sy.wait_ge(S.misc.h, S.misc.n)
            dmaw(sy, S.arw, ar_in.ap()[:, :], zero16[:, :])
        ar_zero_done = S.arw.n

        pa = ExitStack()
        EA = pa.enter_context
        qw_sb = EA(nc.sbuf_tensor("qw_sb", [128, CH, 1536], BF16))
        xts = [EA(nc.sbuf_tensor(f"xts{i}", [128, CH, 512], BF16)) for i in range(2)]
        qkv_sb = EA(nc.sbuf_tensor("qkv_sb", [128, 12, 512], BF16))
        ktd_sb = EA(nc.sbuf_tensor("ktd_sb", [128, 2, 2, 128], BF16))
        vt_sb = EA(nc.sbuf_tensor("vt_sb", [128, 2, 2, 128], BF16))
        sc_bf = EA(nc.sbuf_tensor("sc_bf", [128, 2, 3, 128], BF16))
        tmp_sc = EA(nc.sbuf_tensor("tmp_sc", [128, 2, 256], F32))
        kv_f = EA(nc.sbuf_tensor("kv_f", [128, NH_SP, 128], F32))
        kv_bf = EA(nc.sbuf_tensor("kv_bf", [128, NH_SP, 128], BF16))
        att_sb = [EA(nc.sbuf_tensor(f"att_sb{i}", [128, 2, 512], BF16)) for i in range(2)]
        diag_sb = EA(nc.sbuf_tensor("diag_sb", [128, 8, 2, 256], F32))
        qdec_sb = EA(nc.sbuf_tensor("qdec_sb", [128, 16], F32))
        kdec_sb = EA(nc.sbuf_tensor("kdec_sb", [128, 16], F32))
        bdec_sb = EA(nc.sbuf_tensor("bdec_sb", [128, 8], F32))

        qkvwT3 = qkvwT.rearrange("(c p) n -> p c n", p=128)
        xT3 = xT_b.rearrange("(c p) t -> p c t", p=128)

        @block.gpsimd
        def _(g):
            dmaw(g, S.ldb, diag_sb[:, :, :, :], diag[:, :, :, :])
            dmaw(g, S.ldb, qdec_sb[:, :], qdec[:, :])
            dmaw(g, S.ldb, kdec_sb[:, :], kdec[:, :])
            dmaw(g, S.ldb, bdec_sb[:, :], bdec[:, :])
        decays_done = S.ldb.n

        silu_per_bank = [0, 0]
        uc = [0]
        unit_p = {}
        t_guard = {}
        s_guard = {}
        in_guard = {}
        kv_hs = {}

        for sp in range(2):
            @block.gpsimd
            def _(g, sp=sp):
                if sp == 1:
                    g.wait_ge(S.mm.h, S.mm.n)
                dmaw(g, S.ldq, qw_sb[:, :, :], qkvwT3[:, :, sp * 1536:(sp + 1) * 1536], max_dma_last_dim=6144)

            @block.vector
            def _(v, sp=sp):
                if sp == 1:
                    v.wait_ge(S.im.h, S.im.n)
                else:
                    v.wait_ge(S.ldb.h, decays_done)
                v.memset(kv_f[:, :, :], 0.0)
                i = v.memset(kv_bf[:, :, :], 0.0)
                v.drain()
                S.kv.inc(i, 1)
            for _hs in range(NH_SP):
                kv_hs[_hs] = S.kv.n
            if sp == 1:
                @block.gpsimd
                def _(g):
                    g.wait_ge(S.ast.h, S.ast.n)
                    with allow_ici_all_to_all():
                        i = g.collective_compute(
                            "AllToAll", ALU.bypass,
                            replica_groups=GROUPS_ALL,
                            ins=[a2a_in[0].ap().opt()],
                            outs=[a2a_out[0].ap().opt()],
                        )
                    S.cc.inc(i, 1)

            for tl in range(nblk // 2):
                ti = sp * (nblk // 2) + tl

                @block.gpsimd
                def _(g, tl=tl, ti=ti, buf=ti % 2):
                    if ti >= 2:
                        g.wait_ge(S.mm.h, 12 * (ti - 1))
                    dmaw(g, S.ldx, xts[buf][:, :, :],
                         xT3[:, :, tl * 512:(tl + 1) * 512], max_dma_last_dim=2048)

                for ct in range(12):
                    bank = ct % 2
                    ps = PSQ[bank]
                    guard = silu_per_bank[bank]

                    @block.tensor
                    def _(t, ct=ct, buf=ti % 2, ti=ti, sp=sp, ps=ps, guard=guard):
                        if ct == 0:
                            if ti == 0 and sp == 0:
                                t.wait_ge(S.misc.h, 1)
                            t.wait_ge(S.ldx.h, 16 * (ti + 1))
                            t.wait_ge(S.ldq.h, 16 * (sp + 1))
                        if guard > 0:
                            t.wait_ge(S.ev.h, guard)
                        for ch in range(CH):
                            i = t.matmul(ps[:, 0:512],
                                         qw_sb[:, ch, ct * 128:(ct + 1) * 128],
                                         xts[buf][:, ch, :],
                                         start=(ch == 0), stop=(ch == CH - 1))
                        S.mm.inc(i, 1)

                    @block.scalar
                    def _(sx, ct=ct, ps=ps, first=(ct == 0 and ti == 0)):
                        sx.wait_ge(S.mm.h, S.mm.n)
                        if first:
                            sx.wait_ge(S.ldb.h, decays_done)
                        i = sx.activation(qkv_sb[:, ct, :], ps[:, 0:512], AF.Silu)
                        S.ev.inc(i, 1)
                    silu_per_bank[bank] = S.ev.n

                units = [(bb, hs) for bb in range(2) for hs in range(NH_SP)]
                ae_blk = [0, 0]

                def stage1(un):
                    bb, hs = un
                    hh = sp * NH_SP + hs
                    a_seq = 2 * tl + bb
                    last_blk = (a_seq == nblk - 1)
                    p = uc[0] % 2
                    unit_p[un] = p
                    o = bb * 256
                    q_ap = qkv_sb[:, 3 * hs + 0, o:o + 256]
                    k_ap = qkv_sb[:, 3 * hs + 1, o:o + 256]
                    v_ap = qkv_sb[:, 3 * hs + 2, o:o + 256]
                    pt = ps_tb[p]
                    pss = ps_sb[p]

                    @block.tensor
                    def _(t, hs=hs, p=p, last_blk=last_blk, k_ap=k_ap, q_ap=q_ap,
                          v_ap=v_ap, pt=pt, pss=pss,
                          tg=t_guard.get(p), sg=s_guard.get(p)):
                        t.wait_ge(S.ev.h, S.ev.n - (11 - (3 * hs + 2)))
                        if tg is not None:
                            for sem, val in tg:
                                t.wait_ge(sem.h, val)
                        i = t.transpose(pt[:, 0:128], v_ap[:, 0:128], idb[:, :])
                        i = t.transpose(pt[:, 128:256], v_ap[:, 128:256], idb[:, :])
                        if not last_blk:
                            i = t.transpose(pt[:, 256:384], k_ap[:, 0:128], idb[:, :])
                            i = t.transpose(pt[:, 384:512], k_ap[:, 128:256], idb[:, :])
                        S.tp.inc(i, 1)
                        if sg is not None:
                            for sem, val in sg:
                                t.wait_ge(sem.h, val)
                        for j, (kc, qc) in enumerate([(0, 0), (0, 1), (1, 1)]):
                            i = t.matmul(pss[:, j * 128:(j + 1) * 128],
                                         k_ap[:, kc * 128:(kc + 1) * 128],
                                         q_ap[:, qc * 128:(qc + 1) * 128],
                                         start=True, stop=True)
                        S.sc.inc(i, 1)

                    @block.vector
                    def _(v, p=p, pt=pt):
                        v.wait_ge(S.tp.h, S.tp.n)
                        v.tensor_copy(vt_sb[:, p, 0, :], pt[:, 0:128])
                        i = v.tensor_copy(vt_sb[:, p, 1, :], pt[:, 128:256])
                        S.vt.inc(i, 1)

                    if not last_blk:
                        @block.vector
                        def _(v, hh=hh, p=p, pt=pt):
                            v.wait_ge(S.tp.h, S.tp.n)
                            v.tensor_scalar(ktd_sb[:, p, 0, :], pt[:, 256:384],
                                            kdec_sb[:, 2 * hh + 0:2 * hh + 1], None,
                                            ALU.mult)
                            i = v.tensor_scalar(ktd_sb[:, p, 1, :], pt[:, 384:512],
                                                kdec_sb[:, 2 * hh + 1:2 * hh + 2], None,
                                                ALU.mult)
                            S.ktd.inc(i, 1)

                    @block.vector
                    def _(v, hh=hh, p=p, pss=pss):
                        v.wait_ge(S.sc.h, S.sc.n)
                        for j, (kc, qc) in enumerate([(0, 0), (0, 1), (1, 1)]):
                            i = v.tensor_tensor(
                                sc_bf[:, p, j, :], pss[:, j * 128:(j + 1) * 128],
                                diag_sb[:, hh, kc, qc * 128:(qc + 1) * 128], ALU.mult)
                        S.dk.inc(i, 1)
                    t_guard[p] = [(S.vt, S.vt.n)] + ([(S.ktd, S.ktd.n)] if not last_blk else [])
                    uc[0] += 1

                def stage2(un):
                    bb, hs = un
                    hh = sp * NH_SP + hs
                    a_seq = 2 * tl + bb
                    gb = sp * nblk + a_seq
                    last_blk = (a_seq == nblk - 1)
                    p = unit_p[un]
                    o = bb * 256
                    q_ap = qkv_sb[:, 3 * hs + 0, o:o + 256]
                    pss = ps_sb[p]
                    pin = ps_in[p]
                    blkbuf = a_seq % 2

                    @block.tensor
                    def _(t, hs=hs, p=p, q_ap=q_ap, pss=pss, pin=pin,
                          last_blk=last_blk, ig=in_guard.get(p),
                          kvw=kv_hs.get(hs, 0)):
                        t.wait_ge(S.dk.h, S.dk.n)
                        t.wait_ge(S.vt.h, S.vt.n)
                        if ig is not None:
                            for sem, val in ig:
                                t.wait_ge(sem.h, val)
                        t.matmul(pin[:, 0:128], sc_bf[:, p, 0, :], vt_sb[:, p, 0, :],
                                 start=True, stop=True)
                        t.matmul(pin[:, 128:256], sc_bf[:, p, 1, :], vt_sb[:, p, 0, :],
                                 start=True, stop=False)
                        t.matmul(pin[:, 128:256], sc_bf[:, p, 2, :], vt_sb[:, p, 1, :],
                                 start=False, stop=True)
                        t.wait_ge(S.kv.h, kvw)
                        t.matmul(pin[:, 256:384], q_ap[:, 0:128], kv_bf[:, hs, :],
                                 start=True, stop=True)
                        i = t.matmul(pin[:, 384:512], q_ap[:, 128:256], kv_bf[:, hs, :],
                                     start=True, stop=True)
                        S.im.inc(i, 1)
                        if not last_blk:
                            t.wait_ge(S.ktd.h, S.ktd.n)
                            t.matmul(pss[:, 384:512], ktd_sb[:, p, 0, :], vt_sb[:, p, 0, :],
                                     start=True, stop=False)
                            i = t.matmul(pss[:, 384:512], ktd_sb[:, p, 1, :], vt_sb[:, p, 1, :],
                                         start=False, stop=True)
                            S.umm.inc(i, 1)

                    @block.scalar
                    def _(sx, hh=hh, p=p, pin=pin):
                        sx.wait_ge(S.im.h, S.im.n)
                        for qc in range(2):
                            i = sx.activation(tmp_sc[:, p, qc * 128:(qc + 1) * 128],
                                              pin[:, 256 + qc * 128:256 + (qc + 1) * 128],
                                              AF.Copy,
                                              scale=qdec_sb[:, 2 * hh + qc:2 * hh + qc + 1])
                        S.isc.inc(i, 1)

                    @block.vector
                    def _(v, hs=hs, p=p, blkbuf=blkbuf, gb=gb, pin=pin):
                        v.wait_ge(S.isc.h, S.isc.n)
                        if gb >= 2:
                            v.wait_ge(S.ast.h, 32 * (gb - 1))
                        for qc in range(2):
                            i = v.tensor_tensor(
                                att_sb[blkbuf][:, qc, hs * 128:(hs + 1) * 128],
                                pin[:, qc * 128:(qc + 1) * 128],
                                tmp_sc[:, p, qc * 128:(qc + 1) * 128], ALU.add)
                        S.ae.inc(i, 1)

                    if not last_blk:
                        @block.vector
                        def _(v, hs=hs, hh=hh, p=p, pss=pss):
                            v.wait_ge(S.umm.h, S.umm.n)
                            i = v.scalar_tensor_tensor(
                                kv_f[:, hs, :], kv_f[:, hs, :],
                                bdec_sb[:, hh:hh + 1], pss[:, 384:512],
                                op0=ALU.mult, op1=ALU.add)
                            S.kvp.inc(i, 1)

                        @block.scalar
                        def _(sx, hs=hs):
                            sx.wait_ge(S.kvp.h, S.kvp.n)
                            i = sx.activation(kv_bf[:, hs, :], kv_f[:, hs, :], AF.Copy)
                            S.kv.inc(i, 1)
                        kv_hs[hs] = S.kv.n
                        s_guard[p] = [(S.dk, S.dk.n), (S.kvp, S.kvp.n)]
                    else:
                        s_guard[p] = [(S.dk, S.dk.n), (S.im, S.im.n)]
                    in_guard[p] = [(S.ae, S.ae.n), (S.isc, S.isc.n)]
                    if hs == NH_SP - 1:
                        ae_blk[bb] = S.ae.n

                for step in range(len(units) + 1):
                    if step < len(units):
                        stage1(units[step])
                    if step >= 1:
                        stage2(units[step - 1])

                for bb in range(2):
                    a_seq = 2 * tl + bb

                    @block.sync
                    def _(sy, sp=sp, a_seq=a_seq, blkbuf=a_seq % 2, ae_need=ae_blk[bb]):
                        sy.wait_ge(S.ae.h, ae_need)
                        for half in range(2):
                            dmaw(sy, S.ast,
                                 a2a_in[sp].ap()[a_seq * 256 + half * 128:a_seq * 256 + half * 128 + 128, :],
                                 att_sb[blkbuf][:, half, :])

        @block.gpsimd
        def _(g):
            g.wait_ge(S.ast.h, S.ast.n)
            with allow_ici_all_to_all():
                i = g.collective_compute(
                    "AllToAll", ALU.bypass,
                    replica_groups=GROUPS_ALL,
                    ins=[a2a_in[1].ap().opt()],
                    outs=[a2a_out[1].ap().opt()],
                )
            S.cc.inc(i, 1)
        phase_a_ast = S.ast.n
        qkv_all = S.mm.n

        pa.close()
        pa_psum.close()
        pb_psum = ExitStack()
        px = [pb_psum.enter_context(nc.psum_tensor(f"ps_x{i}", [128, 512], F32)) for i in range(2)]
        pn = pb_psum.enter_context(nc.psum_tensor("ps_pn", [128, 512], F32))

        a2a_r = [a2a_out[i].ap().rearrange("(s t) f -> t s f", s=8) for i in range(2)]
        Amat3 = Amat.rearrange("(c p) n -> p c n", p=128)
        xTl3 = xTloc.rearrange("(c p) t -> p c t", p=128)

        if stage == "attn":
            pb = ExitStack()
            EB = pb.enter_context
            at_t = EB(nc.sbuf_tensor("at_t", [128, 4, 1024], BF16))
            at_f = EB(nc.sbuf_tensor("at_f", [128, 4096], F32))
            for tt in range(8):
                @block.sync
                def _(sy, tt=tt):
                    sy.wait_ge(S.cc.h, 2)
                    if tt >= 1:
                        sy.wait_ge(S.out.h, 16 * tt)
                    cbd, ttd = tt // 4, tt % 4
                    dmaw(sy, S.lat, at_t[:, :, :], a2a_r[ttd * 128:(ttd + 1) * 128, 4 * cbd:4 * cbd + 4, :])

                @block.vector
                def _(v):
                    v.wait_ge(S.lat.h, S.lat.n)
                    i = v.tensor_copy(at_f[:, :], at_t.rearrange("p s f -> p (s f)")[:, :])
                    S.misc.inc(i, 1)

                @block.sync
                def _(sy, tt=tt):
                    sy.wait_ge(S.misc.h, S.misc.n)
                    dmaw(sy, S.out, out_e[tt * 128:(tt + 1) * 128, :], at_f[:, :])

            @block.sync
            def _(sy):
                sy.wait_ge(S.out.h, S.out.n)
            pb.close()
            pb_psum.close()
            return nc

        pb = ExitStack()
        EB = pb.enter_context
        big1 = EB(nc.sbuf_tensor("big1", [128, 16384], BF16))
        at_v = lambda tt: big1[:, tt * 4096:(tt + 1) * 4096]
        gnT_v = lambda fc, t0, t1: big1[:, fc * 512 + t0:fc * 512 + t1]
        xx_sb = EB(nc.sbuf_tensor("xx_sb", [128, CH, 512], BF16))
        u1 = EB(nc.sbuf_tensor("u1", [128, HID], F32))
        wbuf = [EB(nc.sbuf_tensor(f"wbuf{i}", [128, CH, 256], BF16)) for i in range(2)]
        BmB = EB(nc.sbuf_tensor("BmB", [33, HID], BF16))
        nw_sb = EB(nc.sbuf_tensor("nw_sb", [128, 32], F32))
        b8_sb = EB(nc.sbuf_tensor("b8_sb", [8, 1], F32))
        A_bf = EB(nc.sbuf_tensor("A_bf", [128, CH, 32], BF16))
        W8_bf = EB(nc.sbuf_tensor("W8_bf", [128, CH, 8], BF16))
        lim_bf = EB(nc.sbuf_tensor("lim_bf", [8, 32], BF16))
        limT_bf = EB(nc.sbuf_tensor("limT_bf", [32, 8], BF16))
        zT_sb = [EB(nc.sbuf_tensor(f"zT_sb{i}", [32, 512], BF16)) for i in range(2)]
        zpT_sb = [EB(nc.sbuf_tensor(f"zpT_sb{i}", [32, 512], BF16)) for i in range(2)]
        hsA = [EB(nc.sbuf_tensor(f"hsA{i}", [8, 256], F32)) for i in range(2)]
        dsA = [EB(nc.sbuf_tensor(f"dsA{i}", [8, 256], F32)) for i in range(2)]
        mx8 = EB(nc.sbuf_tensor("mx8", [8, 4], F32))
        mx2 = EB(nc.sbuf_tensor("mx2", [8, 2], F32))
        arv = EB(nc.sbuf_tensor("arv", [2, 8], F32))
        ar2 = EB(nc.sbuf_tensor("ar2", [2, 1], F32))
        arr = EB(nc.sbuf_tensor("arr", [1, 16], F32))
        inv2 = EB(nc.sbuf_tensor("inv2", [1, 2], F32))
        invb = EB(nc.sbuf_tensor("invb", [128, 2], F32))
        stats = EB(nc.sbuf_tensor("stats", [128, 32], F32))
        s2buf = EB(nc.sbuf_tensor("s2buf", [128, 64], F32))
        q2buf = EB(nc.sbuf_tensor("q2buf", [128, 64], F32))
        dump = EB(nc.sbuf_tensor("dump", [128, 256], BF16))
        wfT = [EB(nc.sbuf_tensor(f"wfT{i}", [8, 512], BF16)) for i in range(2)]
        hs_tm = EB(nc.sbuf_tensor("hs_tm", [128, 2, 8], F32))
        ds_tm = EB(nc.sbuf_tensor("ds_tm", [128, 2, 8], F32))
        rt = EB(nc.sbuf_tensor("rt", [128, 10, 8], F32))
        rt1 = EB(nc.sbuf_tensor("rt1", [128, 8], F32))
        rmsk = EB(nc.sbuf_tensor("rmsk", [128, 8], I32))
        neg8 = EB(nc.sbuf_tensor("neg8", [128, 8], F32))
        Ybuf = EB(nc.sbuf_tensor("Ybuf", [128, 4, HID], F32))

        @block.gpsimd
        def _(g):
            g.wait_ge(S.mm.h, qkv_all)
            dmaw(g, S.ldb, nw_sb[:, :], nwcol[:, :])
            dmaw(g, S.ldb, b8_sb[:, :], b8[:, :])
            dmaw(g, S.ldb, A_bf[:, :, :], Amat3[:, :, :])
            dmaw(g, S.ldb, W8_bf[:, :, :], WT[0][:, :, 0:8])
            dmaw(g, S.ldb, BmB[:, :], Bm[:, :], max_dma_last_dim=8192)
            dmaw(g, S.ldb, lim_bf[:, :], limes[:, :])
            dmaw(g, S.ldb, limT_bf[:, :], limesT[:, :])
        b_loads = S.ldb.n

        @block.vector
        def _(v):
            v.memset(neg8[:, :], -1e30)
            v.drain()

        bank_guard = [None, None]
        wbuf_guard = [None, None]
        fslot_guard = {}

        def wait_bank(t, bank):
            if bank_guard[bank] is not None:
                sem, val = bank_guard[bank]
                t.wait_ge(sem.h, val)

        def wait_wbuf(g, wb):
            if wbuf_guard[wb] is not None:
                sem, val = wbuf_guard[wb]
                g.wait_ge(sem.h, val)

        def rsqrt(s, dst, src, scale, bias):
            s.drain()
            s.activation(dst, src, AF.Ln, scale=scale, bias=bias)
            s.drain()
            i = s.activation(dst, dst, AF.Exp, scale=-0.5)
            s.drain()
            return i

        fgrp_state = [0]

        def emit_f32_transposes(src_fn, dst_fn, nw_scale, extra_first_waits=(),
                                per_tt_waits=None):
            first = True
            tiles = [(tt, fc) for tt in range(4) for fc in range(CH)]
            tt_marks = [0, 0, 0, 0]
            for g0 in range(0, len(tiles), 4):
                grp = tiles[g0:g0 + 4]
                bank = (g0 // 4) % 2
                psb = px[bank]
                ttw = None
                if per_tt_waits is not None and grp[0][1] == 0:
                    ttw = per_tt_waits[grp[0][0]]

                @block.tensor
                def _(t, grp=grp, psb=psb, first=first, ttw=ttw,
                      guard=fslot_guard.get(bank), xw=tuple(extra_first_waits)):
                    if first:
                        for sem, val in xw:
                            t.wait_ge(sem.h, val)
                    if ttw is not None:
                        t.wait_ge(ttw[0].h, ttw[1])
                    if guard is not None:
                        t.wait_ge(guard[0].h, guard[1])
                    for k, (tt, fc) in enumerate(grp):
                        i = t.transpose(psb[:, k * 128:(k + 1) * 128],
                                        src_fn(tt, fc), idf[:, :])
                    S.bt.inc(i, 1)

                if bank == 0:
                    @block.vector
                    def _(v, grp=grp, psb=psb, nw=nw_scale):
                        v.wait_ge(S.bt.h, S.bt.n)
                        for k, (tt, fc) in enumerate(grp):
                            if nw is None:
                                i = v.tensor_copy(dst_fn(tt, fc),
                                                  psb[:, k * 128:(k + 1) * 128])
                            else:
                                i = v.tensor_scalar(dst_fn(tt, fc),
                                                    psb[:, k * 128:(k + 1) * 128],
                                                    nw[:, fc:fc + 1], None, ALU.mult)
                        S.bte.inc(i, 1)
                    fslot_guard[bank] = (S.bte, S.bte.n)
                else:
                    @block.scalar
                    def _(s, grp=grp, psb=psb, nw=nw_scale):
                        s.wait_ge(S.bt.h, S.bt.n)
                        for k, (tt, fc) in enumerate(grp):
                            if nw is None:
                                i = s.activation(dst_fn(tt, fc),
                                                 psb[:, k * 128:(k + 1) * 128], AF.Copy)
                            else:
                                i = s.activation(dst_fn(tt, fc),
                                                 psb[:, k * 128:(k + 1) * 128], AF.Copy,
                                                 scale=nw[:, fc:fc + 1])
                        S.bts.inc(i, 1)
                    fslot_guard[bank] = (S.bts, S.bts.n)
                tt_marks[grp[-1][0]] = ((S.bte, S.bte.n), (S.bts, S.bts.n))
                first = False
            return tt_marks

        for cb in range(2):
            @block.gpsimd
            def _(g, cb=cb):
                if cb == 1:
                    g.wait_ge(S.bmm.h, S.bmm.n)
                else:
                    g.wait_ge(S.mm.h, qkv_all)
                dmaw(g, S.ldb, xx_sb[:, :, :], xTl3[:, :, cb * 512:(cb + 1) * 512], max_dma_last_dim=2048)
            xg_done = S.ldb.n

            for tt in range(4):
                @block.sync
                def _(sy, cb=cb, tt=tt):
                    sy.wait_ge(S.cc.h, 2)
                    if cb == 1:
                        sy.wait_ge(S.sq.h, S.sq.n)
                        sy.wait_ge(S.bte.h, S.bte.n)
                        sy.wait_ge(S.bts.h, S.bts.n)
                        sy.wait_ge(S.bmm.h, S.bmm.n)
                        sy.wait_ge(S.nrm.h, S.nrm.n)
                    atv4 = at_v(tt).rearrange("p (sh h f) -> p sh h f", sh=4, h=2)
                    for hlf in range(2):
                        dmaw(sy, S.lat, atv4[:, :, hlf, :],
                             a2a_r[hlf][tt * 128:(tt + 1) * 128, 4 * cb:4 * cb + 4, :])

            nrm_tt = [None] * 4
            x2_tt = [None] * 4
            for cg in range(16 if bstop >= 1 else 0):
                wb = cg % 2

                @block.gpsimd
                def _(g, cb=cb, cg=cg, wb=wb):
                    wait_wbuf(g, wb)
                    if cb == 0 and cg < 2:
                        g.wait_ge(S.mm.h, qkv_all)
                    dmaw(g, S.ldw, wbuf[wb][:, :, :], gatewT[cg], max_dma_last_dim=8192)

                for tt in range(4):
                    gi = cg * 4 + tt
                    bank = gi % 2
                    ps = PSQ[bank]

                    @block.tensor
                    def _(t, cg=cg, tt=tt, wb=wb, ps=ps, bank=bank):
                        t.wait_ge(S.ldw.h, S.ldw.n)
                        if cg == 0 and tt == 0:
                            t.wait_ge(S.ldb.h, xg_done)
                        wait_bank(t, bank)
                        for ch in range(CH):
                            i = t.matmul(ps[:, 0:256],
                                         xx_sb[:, ch, tt * 128:(tt + 1) * 128],
                                         wbuf[wb][:, ch, :],
                                         start=(ch == 0), stop=(ch == CH - 1))
                        S.bmm.inc(i, 1)

                    @block.scalar
                    def _(s, cb=cb, cg=cg, tt=tt, ps=ps):
                        s.wait_ge(S.bmm.h, S.bmm.n)
                        if cb == 0 and cg == 0 and tt == 0:
                            s.wait_ge(S.ast.h, phase_a_ast)
                        if cb == 1 and cg == 0 and tt == 0 and bstop >= 6:
                            s.wait_ge(S.stx.h, 64)
                        i = s.activation(Ybuf[:, tt, cg * 256:(cg + 1) * 256],
                                         ps[:, 0:256], AF.Sigmoid)
                        S.sig.inc(i, 1)
                    bank_guard[bank] = (S.sig, S.sig.n)
                wbuf_guard[wb] = (S.bmm, S.bmm.n)

            for tt in range(4):
                @block.scalar
                def _(s, cb=cb, tt=tt):
                    s.wait_ge(S.lat.h, 32 * (cb * 4 + tt + 1))
                    if cb == 0 and tt == 0:
                        s.wait_ge(S.ldb.h, b_loads)
                    s.activation(u1[:, :], at_v(tt), AF.Square,
                                 accum_out=stats[:, tt:tt + 1])
                    i = rsqrt(s, stats[:, 4 + tt:5 + tt], stats[:, tt:tt + 1],
                              1.0 / HID, EPS_ATTN)
                    S.ln.inc(i, 1)

                @block.vector
                def _(v, tt=tt):
                    v.wait_ge(S.ln.h, S.ln.n)
                    v.wait_ge(S.sig.h, S.sig.n)
                    i = v.scalar_tensor_tensor(Ybuf[:, tt, :], at_v(tt),
                                               stats[:, 4 + tt:5 + tt], Ybuf[:, tt, :],
                                               op0=ALU.mult, op1=ALU.mult)
                    S.nrm.inc(i, 1)
                nrm_tt[tt] = (S.nrm, S.nrm.n)

            gnT_tt = [None] * 4
            if bstop >= 2:
                gnT_tt = emit_f32_transposes(
                    src_fn=lambda tt, fc: Ybuf[:, tt, fc * 128:(fc + 1) * 128],
                    dst_fn=lambda tt, fc: gnT_v(fc, tt * 128, (tt + 1) * 128),
                    nw_scale=nw_sb,
                    per_tt_waits=nrm_tt)
            gnT_done = S.bte.n

            for cg in range(16 if bstop >= 3 else 0):
                wb = cg % 2

                @block.gpsimd
                def _(g, cg=cg, wb=wb):
                    wait_wbuf(g, wb)
                    dmaw(g, S.ldw, wbuf[wb][:, :, :], outwT[cg], max_dma_last_dim=8192)

                for tt in range(4):
                    gi = cg * 4 + tt
                    bank = gi % 2
                    ps = PSQ[bank]

                    @block.tensor
                    def _(t, cg=cg, tt=tt, wb=wb, ps=ps, bank=bank):
                        t.wait_ge(S.ldw.h, S.ldw.n)
                        if cg == 0 and gnT_tt[tt] is not None:
                            for sem, val in gnT_tt[tt]:
                                t.wait_ge(sem.h, val)
                        wait_bank(t, bank)
                        for ch in range(CH):
                            i = t.matmul(ps[:, 0:256],
                                         gnT_v(ch, tt * 128, (tt + 1) * 128),
                                         wbuf[wb][:, ch, :],
                                         start=(ch == 0), stop=(ch == CH - 1))
                        S.bmm.inc(i, 1)

                    @block.vector
                    def _(v, cg=cg, tt=tt, ps=ps):
                        v.wait_ge(S.bmm.h, S.bmm.n)
                        i = v.tensor_copy(Ybuf[:, tt, cg * 256:(cg + 1) * 256], ps[:, 0:256])
                        S.ye.inc(i, 1)
                    bank_guard[bank] = (S.ye, S.ye.n)
                wbuf_guard[wb] = (S.bmm, S.bmm.n)
            oproj_done = S.ye.n

            for tt in range(4 if bstop >= 4 else 0):
                @block.sync
                def _(sy, cb=cb, tt=tt):
                    sy.wait_ge(S.ye.h, S.ye.n)
                    dmaw(sy, S.xr, u1[:, :], x_rows[cb * 512 + tt * 128:cb * 512 + (tt + 1) * 128, :])

                @block.vector
                def _(v, tt=tt):
                    v.wait_ge(S.xr.h, S.xr.n)
                    v.drain()
                    i = v.scalar_tensor_tensor(Ybuf[:, tt, :], Ybuf[:, tt, :], 1.0,
                                               u1[:, :], op0=ALU.mult, op1=ALU.add,
                                               accum_out=stats[:, 8 + tt:9 + tt])
                    S.ye.inc(i, 1)

                @block.scalar
                def _(s, tt=tt):
                    s.wait_ge(S.ye.h, S.ye.n)
                    i = s.activation(at_v(tt), Ybuf[:, tt, :], AF.Square,
                                     accum_out=stats[:, 12 + tt:13 + tt])
                    S.sq.inc(i, 1)

                @block.vector
                def _(v, tt=tt):
                    v.wait_ge(S.sq.h, S.sq.n)
                    v.drain()
                    v.tensor_scalar(stats[:, 16 + tt:17 + tt], stats[:, 8 + tt:9 + tt],
                                    1.0 / HID, None, ALU.mult)
                    v.tensor_scalar(stats[:, 20 + tt:21 + tt], stats[:, 12 + tt:13 + tt],
                                    1.0 / HID, None, ALU.mult)
                    v.drain()
                    v.tensor_tensor(stats[:, 24 + tt:25 + tt],
                                    stats[:, 16 + tt:17 + tt],
                                    stats[:, 16 + tt:17 + tt], ALU.mult)
                    v.drain()
                    i = v.tensor_tensor(stats[:, 20 + tt:21 + tt],
                                        stats[:, 20 + tt:21 + tt],
                                        stats[:, 24 + tt:25 + tt], ALU.subtract)
                    S.misc.inc(i, 1)

                @block.scalar
                def _(s, tt=tt):
                    s.wait_ge(S.misc.h, S.misc.n)
                    i = rsqrt(s, stats[:, 24 + tt:25 + tt], stats[:, 20 + tt:21 + tt],
                              1.0, EPS_LN)
                    S.ln.inc(i, 1)

                @block.vector
                def _(v, tt=tt):
                    v.wait_ge(S.ln.h, S.ln.n)
                    v.drain()
                    i = v.tensor_scalar(Ybuf[:, tt, :], Ybuf[:, tt, :],
                                        stats[:, 16 + tt:17 + tt],
                                        stats[:, 24 + tt:25 + tt],
                                        ALU.subtract, op1=ALU.mult)
                    S.ye.inc(i, 1)
                x2_tt[tt] = (S.ye, S.ye.n)
            x2_done = S.ye.n

            if bstop >= 5:
                emit_f32_transposes(
                    src_fn=lambda tt, fc: Ybuf[:, tt, fc * 128:(fc + 1) * 128],
                    dst_fn=lambda tt, fc: xx_sb[:, fc, tt * 128:(tt + 1) * 128],
                    nw_scale=None,
                    extra_first_waits=[(S.bmm, S.bmm.n)],
                    per_tt_waits=x2_tt)
            x2T_done = (S.bte.n, S.bts.n)

            if bstop < 6:
                continue
            @block.tensor
            def _(t, cb=cb):
                t.wait_ge(S.bte.h, x2T_done[0])
                t.wait_ge(S.bts.h, x2T_done[1])
                t.wait_ge(S.misc.h, S.misc.n)
                t.wait_ge(S.ldb.h, b_loads)
                for ch in range(CH):
                    i = t.matmul(pn[0:32, 0:512], A_bf[:, ch, :],
                                 xx_sb[:, ch, :],
                                 start=(ch == 0), stop=(ch == CH - 1))
                S.bmm.inc(i, 1)

            @block.vector
            def _(v, cb=cb):
                v.wait_ge(S.bmm.h, S.bmm.n)
                i = v.tensor_copy(zT_sb[cb][:, :], pn[0:32, 0:512])
                S.misc.inc(i, 1)
            zt_done = S.misc.n

            @block.tensor
            def _(t, cb=cb):
                t.wait_ge(S.misc.h, zt_done)
                for ch in range(CH):
                    i = t.matmul(pn[0:8, 0:512], W8_bf[:, ch, :],
                                 xx_sb[:, ch, :],
                                 start=(ch == 0), stop=(ch == CH - 1))
                S.bmm.inc(i, 1)

            @block.vector
            def _(v, cb=cb):
                v.wait_ge(S.bmm.h, S.bmm.n)
                v.tensor_scalar(hsA[cb][:, :], pn[0:8, 1:512:2], b8_sb[:, :],
                                None, ALU.add)
                v.drain()
                i = v.tensor_reduce(mx8[:, cb:cb + 1], hsA[cb][:, :], axis=AX,
                                    op=ALU.max, apply_absolute_value=True)
                S.misc.inc(i, 1)
            hs_done = S.misc.n

            @block.tensor
            def _(t, cb=cb):
                t.wait_ge(S.misc.h, hs_done)
                i = t.matmul(pn[0:8, 0:512], limT_bf[:, :], zT_sb[cb][:, :],
                             start=True, stop=True)
                S.bmm.inc(i, 1)

            @block.vector
            def _(v, cb=cb):
                v.wait_ge(S.bmm.h, S.bmm.n)
                v.tensor_copy(dsA[cb][:, :], pn[0:8, 1:512:2])
                v.drain()
                i = v.tensor_reduce(mx8[:, 2 + cb:3 + cb], dsA[cb][:, :], axis=AX,
                                    op=ALU.max, apply_absolute_value=True)
                S.misc.inc(i, 1)

            if cb == 0:
                for tt in range(4):
                    @block.sync
                    def _(sy, tt=tt):
                        sy.wait_ge(S.bte.h, x2T_done[0])
                        sy.wait_ge(S.bts.h, x2T_done[1])
                        dmaw(sy, S.stx, x2stash.ap()[tt * 128:(tt + 1) * 128, :], Ybuf[:, tt, :])

        router_done = [0]
        b_row = big1[0:1, 0:HID]
        if bstop >= 9:
            @block.gpsimd
            def _(g):
                g.wait_ge(S.sq.h, S.sq.n)
                g.wait_ge(S.nrm.h, S.nrm.n)
                dmaw(g, S.ldb, b_row, Bm[32:33, :], max_dma_last_dim=8192)
        b_row_ld = S.ldb.n

        def b2_pass1(cb, x2T_ready):
            for cg in range(16):
                wb = cg % 2

                @block.gpsimd
                def _(g, cg=cg, wb=wb):
                    wait_wbuf(g, wb)
                    dmaw(g, S.ldw, wbuf[wb][:, :, :], WT[cg], max_dma_last_dim=8192)

                for tt in range(4):
                    bank = (cg * 4 + tt) % 2
                    ps = PSQ[bank]

                    @block.tensor
                    def _(t, cb=cb, cg=cg, tt=tt, wb=wb, ps=ps, bank=bank,
                          x2T_ready=x2T_ready):
                        t.wait_ge(S.ldw.h, S.ldw.n)
                        if cg == 0 and tt == 0:
                            t.wait_ge(S.bte.h, x2T_ready[0])
                            t.wait_ge(S.bts.h, x2T_ready[1])
                            t.wait_ge(S.ldb.h, b_row_ld)
                        wait_bank(t, bank)
                        t.matmul(ps[:, 0:256], ones_bf[0:1, :],
                                 b_row[:, cg * 256:(cg + 1) * 256],
                                 start=True, stop=False)
                        for ch in range(CH):
                            i = t.matmul(ps[:, 0:256],
                                         xx_sb[:, ch, tt * 128:(tt + 1) * 128],
                                         wbuf[wb][:, ch, :],
                                         start=False, stop=(ch == CH - 1))
                        S.bmm.inc(i, 1)

                    @block.vector
                    def _(v, cg=cg, tt=tt, ps=ps):
                        v.wait_ge(S.bmm.h, S.bmm.n)
                        i = v.scalar_tensor_tensor(
                            Ybuf[:, tt, cg * 256:(cg + 1) * 256], ps[:, 0:256], 1.0,
                            Ybuf[:, tt, cg * 256:(cg + 1) * 256],
                            op0=ALU.mult, op1=ALU.add)
                        S.ye.inc(i, 1)
                    bank_guard[bank] = (S.ye, S.ye.n)
                wbuf_guard[wb] = (S.bmm, S.bmm.n)

        def b2_pass2(cb):
            ye_mark = [0, 0, 0, 0]
            for cg in range(16):
                for tt in range(4):
                    bank = (cg * 4 + tt) % 2
                    ps = PSQ[bank]

                    @block.tensor
                    def _(t, cb=cb, cg=cg, tt=tt, ps=ps, bank=bank,
                          first=(cg == 0 and tt == 0)):
                        if first:
                            t.wait_ge(S.misc.h, router_done[0])
                        wait_bank(t, bank)
                        i = t.matmul(ps[:, 0:256],
                                     zpT_sb[cb][:, tt * 128:(tt + 1) * 128],
                                     BmB[0:32, cg * 256:(cg + 1) * 256],
                                     start=True, stop=True)
                        S.bmm.inc(i, 1)

                    @block.vector
                    def _(v, cg=cg, tt=tt, ps=ps):
                        v.wait_ge(S.bmm.h, S.bmm.n)
                        i = v.scalar_tensor_tensor(
                            Ybuf[:, tt, cg * 256:(cg + 1) * 256], ps[:, 0:256], 1.0,
                            Ybuf[:, tt, cg * 256:(cg + 1) * 256],
                            op0=ALU.mult, op1=ALU.add,
                            accum_out=s2buf[:, tt * 16 + cg:tt * 16 + cg + 1])
                        S.ye.inc(i, 1)
                    bank_guard[bank] = (S.ye, S.ye.n)
                    if cg == 15:
                        ye_mark[tt] = S.ye.n

            for tt in range(4):
                @block.scalar
                def _(s, tt=tt):
                    s.wait_ge(S.ye.h, ye_mark[tt])
                    s.drain()
                    i = s.activation(u1[:, :], Ybuf[:, tt, :], AF.Square,
                                     accum_out=q2buf[:, tt * 16:tt * 16 + 1])
                    S.sq.inc(i, 1)

        def b2_ln2out(cb):
            for tt in range(4):
                @block.vector
                def _(v, tt=tt):
                    v.wait_ge(S.sq.h, S.sq.n)
                    v.drain()
                    v.tensor_reduce(stats[:, 8 + tt:9 + tt], s2buf[:, tt * 16:(tt + 1) * 16],
                                    axis=AX, op=ALU.add)
                    v.tensor_copy(stats[:, 12 + tt:13 + tt], q2buf[:, tt * 16:tt * 16 + 1])
                    v.drain()
                    v.tensor_scalar(stats[:, 16 + tt:17 + tt], stats[:, 8 + tt:9 + tt],
                                    1.0 / HID, None, ALU.mult)
                    v.tensor_scalar(stats[:, 20 + tt:21 + tt], stats[:, 12 + tt:13 + tt],
                                    1.0 / HID, None, ALU.mult)
                    v.drain()
                    v.tensor_tensor(stats[:, 24 + tt:25 + tt],
                                    stats[:, 16 + tt:17 + tt],
                                    stats[:, 16 + tt:17 + tt], ALU.mult)
                    v.drain()
                    i = v.tensor_tensor(stats[:, 20 + tt:21 + tt],
                                        stats[:, 20 + tt:21 + tt],
                                        stats[:, 24 + tt:25 + tt], ALU.subtract)
                    S.misc.inc(i, 1)

                @block.scalar
                def _(s, tt=tt):
                    s.wait_ge(S.misc.h, S.misc.n)
                    i = rsqrt(s, stats[:, 24 + tt:25 + tt], stats[:, 20 + tt:21 + tt],
                              1.0, EPS_LN)
                    S.ln.inc(i, 1)

                @block.vector
                def _(v, tt=tt):
                    v.wait_ge(S.ln.h, S.ln.n)
                    v.drain()
                    i = v.tensor_scalar(Ybuf[:, tt, :], Ybuf[:, tt, :],
                                        stats[:, 16 + tt:17 + tt],
                                        stats[:, 24 + tt:25 + tt],
                                        ALU.subtract, op1=ALU.mult)
                    S.misc.inc(i, 1)

                @block.sync
                def _(sy, cb=cb, tt=tt):
                    sy.wait_ge(S.misc.h, S.misc.n)
                    dmaw(sy, S.out, out_e[cb * 512 + tt * 128:cb * 512 + (tt + 1) * 128, :],
                         Ybuf[:, tt, :])

        if bstop < 7:
            pb.close()
            pb_psum.close()
            return nc
        @block.vector
        def _(v):
            v.wait_ge(S.misc.h, S.misc.n)
            v.drain()
            v.tensor_tensor(mx2[:, 0:1], mx8[:, 0:1], mx8[:, 1:2], ALU.max)
            v.drain()
            i = v.tensor_tensor(mx2[:, 1:2], mx8[:, 2:3], mx8[:, 3:4], ALU.max)
            S.misc.inc(i, 1)

        @block.tensor
        def _(t):
            t.wait_ge(S.misc.h, S.misc.n)
            i = t.transpose(px[0][0:2, 0:8], mx2[0:8, 0:2], idf[0:8, 0:8])
            S.bt.inc(i, 1)

        @block.vector
        def _(v):
            v.wait_ge(S.bt.h, S.bt.n)
            v.tensor_copy(arv[:, :], px[0][0:2, 0:8])
            v.drain()
            i = v.tensor_reduce(ar2[:, :], arv[:, :], axis=AX, op=ALU.max)
            S.misc.inc(i, 1)

        @block.sync
        def _(sy):
            sy.wait_ge(S.misc.h, S.misc.n)
            sy.wait_ge(S.arw.h, ar_zero_done)
            dmaw(sy, S.arw, ar_in.ap()[0:1, 0:2], ar2[0:2, 0:1])

        @block.gpsimd
        def _(g):
            g.wait_ge(S.arw.h, S.arw.n)
            i = g.collective_compute(
                "AllReduce", ALU.max,
                replica_groups=GROUPS_ALL,
                ins=[ar_in.ap().opt()],
                outs=[ar_out.ap().opt()],
            )
            S.cc.inc(i, 1)
        ar_cc = S.cc.n

        @block.sync
        def _(sy):
            sy.wait_ge(S.cc.h, ar_cc)
            dmaw(sy, S.arw, arr[:, :], ar_out.ap()[:, :])

        @block.vector
        def _(v):
            v.wait_ge(S.arw.h, S.arw.n)
            v.tensor_scalar(inv2[:, :], arr[0:1, 0:2], 1e-6, None, ALU.max)
            v.drain()
            v.reciprocal(inv2[:, :], inv2[:, :])
            v.drain()
            i = v.tensor_scalar(inv2[:, :], inv2[:, :], 0.5, None, ALU.mult)
            S.misc.inc(i, 1)


        if bstop >= 9:
            b2_pass1(1, x2T_ready=(S.bte.n, S.bts.n))
        @block.tensor
        def _(t):
            t.wait_ge(S.misc.h, S.misc.n)
            i = t.matmul(pn[:, 256:258], onesf[:, :], inv2[:, :], start=True, stop=True)
            S.bt.inc(i, 1)

        @block.vector
        def _(v):
            v.wait_ge(S.bt.h, S.bt.n)
            i = v.tensor_copy(invb[:, :], pn[:, 256:258])
            v.drain()
            S.misc.inc(i, 1)
        invb_done = S.misc.n

        for cb in range(2 if bstop >= 8 else 0):
            for at in range(2):
                @block.tensor
                def _(t, cb=cb, at=at):
                    t.wait_ge(S.misc.h, S.misc.n)
                    t.transpose(px[0][0:128, 0:8], hsA[cb][:, at * 128:(at + 1) * 128],
                                idf[0:8, 0:8])
                    i = t.transpose(px[0][0:128, 8:16], dsA[cb][:, at * 128:(at + 1) * 128],
                                    idf[0:8, 0:8])
                    S.bt.inc(i, 1)

                @block.vector
                def _(v, at=at):
                    v.wait_ge(S.bt.h, S.bt.n)
                    v.tensor_copy(hs_tm[:, at, :], px[0][0:128, 0:8])
                    v.tensor_copy(ds_tm[:, at, :], px[0][0:128, 8:16])
                    v.drain()
                    v.tensor_scalar(rt[:, 0, :], hs_tm[:, at, :], invb[:, 0:1], None, ALU.mult)
                    v.drain()
                    i = v.scalar_tensor_tensor(rt[:, 1, :], ds_tm[:, at, :], invb[:, 1:2],
                                               rt[:, 0, :], op0=ALU.mult, op1=ALU.add)
                    S.misc.inc(i, 1)

                @block.scalar
                def _(s):
                    s.wait_ge(S.misc.h, S.misc.n)
                    i = s.activation(rt[:, 2, :], rt[:, 1, :], AF.Exp)
                    S.ln.inc(i, 1)

                @block.vector
                def _(v):
                    v.wait_ge(S.ln.h, S.ln.n)
                    v.tensor_reduce(rt1[:, 0:1], rt[:, 2, :], axis=AX, op=ALU.add)
                    v.drain()
                    v.reciprocal(rt1[:, 1:2], rt1[:, 0:1])
                    v.drain()
                    v.tensor_scalar(rt[:, 3, :], rt[:, 2, :], rt1[:, 1:2], None, ALU.mult)
                    v.drain()
                    v.tensor_reduce(rt1[:, 2:3], rt[:, 3, :], axis=AX, op=ALU.max)
                    v.drain()
                    v.tensor_scalar(rmsk[:, :], rt[:, 3, :], rt1[:, 2:3], None, ALU.is_lt)
                    v.drain()
                    v.tensor_copy(rt[:, 4, :], neg8[:, :])
                    v.drain()
                    v.copy_predicated(rt[:, 4, :], rmsk[:, :], rt[:, 3, :])
                    v.drain()
                    v.tensor_reduce(rt1[:, 3:4], rt[:, 4, :], axis=AX, op=ALU.max)
                    v.tensor_scalar(rt[:, 5, :], rt[:, 3, :], rt1[:, 2:3], None, ALU.is_ge)
                    v.drain()
                    v.tensor_reduce(rt1[:, 4:5], rt[:, 5, :], axis=AX, op=ALU.add)
                    v.drain()
                    v.tensor_scalar(rmsk[:, 0:1], rt1[:, 4:5], 1.5, None, ALU.is_gt)
                    v.drain()
                    v.tensor_copy(rt1[:, 5:6], rt1[:, 3:4])
                    v.drain()
                    v.copy_predicated(rt1[:, 5:6], rmsk[:, 0:1], rt1[:, 2:3])
                    v.drain()
                    i = v.tensor_scalar(rt1[:, 6:7], rt1[:, 5:6], 2.0, None, ALU.mult)
                    S.misc.inc(i, 1)

                @block.scalar
                def _(s):
                    s.wait_ge(S.misc.h, S.misc.n)
                    i = s.activation(rt[:, 6, :], rt[:, 3, :], AF.Exp, scale=-2.0,
                                     bias=rt1[:, 6:7])
                    S.ln.inc(i, 1)

                @block.vector
                def _(v):
                    v.wait_ge(S.ln.h, S.ln.n)
                    v.tensor_scalar(rt[:, 7, :], rt[:, 6, :], 1.0, None, ALU.add)
                    v.drain()
                    v.reciprocal(rt[:, 8, :], rt[:, 7, :])
                    v.drain()
                    v.tensor_tensor(rt[:, 9, :], rt[:, 3, :], rt[:, 8, :], ALU.mult)
                    v.drain()
                    v.tensor_reduce(rt1[:, 7:8], rt[:, 9, :], axis=AX, op=ALU.add)
                    v.drain()
                    v.tensor_scalar(rt1[:, 4:5], rt1[:, 7:8], 1e-9, None, ALU.add)
                    v.drain()
                    v.reciprocal(rt1[:, 5:6], rt1[:, 4:5])
                    v.drain()
                    i = v.tensor_scalar(rt[:, 0, :], rt[:, 9, :], rt1[:, 5:6], None, ALU.mult)
                    v.drain()
                    S.misc.inc(i, 1)

                @block.tensor
                def _(t):
                    t.wait_ge(S.misc.h, S.misc.n)
                    i = t.transpose(px[0][0:8, 0:128], rt[:, 0, :], idf[:, :])
                    S.bt.inc(i, 1)

                @block.vector
                def _(v, cb=cb, at=at):
                    v.wait_ge(S.bt.h, S.bt.n)
                    v.tensor_copy(wfT[cb][:, at * 256 + 0:at * 256 + 256:2], px[0][0:8, 0:128])
                    i = v.tensor_copy(wfT[cb][:, at * 256 + 1:at * 256 + 256:2], px[0][0:8, 0:128])
                    S.misc.inc(i, 1)
            wf_done = S.misc.n

            @block.tensor
            def _(t, cb=cb):
                t.wait_ge(S.misc.h, wf_done)
                i = t.matmul(pn[0:32, 0:512], lim_bf[:, :], wfT[cb][:, :],
                             start=True, stop=True)
                S.bmm.inc(i, 1)

            @block.vector
            def _(v, cb=cb):
                v.wait_ge(S.bmm.h, S.bmm.n)
                v.drain()
                i = v.scalar_tensor_tensor(zpT_sb[cb][:, :], zT_sb[cb][:, :], 0.5,
                                           pn[0:32, 0:512], op0=ALU.mult, op1=ALU.mult)
                S.misc.inc(i, 1)
        router_done[0] = S.misc.n

        if bstop >= 9:
            b2_pass2(1)
            b2_ln2out(1)

            rld_tt = [None] * 4
            for tt in range(4):
                @block.sync
                def _(sy, tt=tt):
                    sy.wait_ge(S.out.h, 16 * (tt + 1))
                    dmaw(sy, S.stx, Ybuf[:, tt, :], x2stash.ap()[tt * 128:(tt + 1) * 128, :])
                rld_tt[tt] = (S.stx, S.stx.n)
            stx_reload = S.stx.n
            emit_f32_transposes(
                src_fn=lambda tt, fc: Ybuf[:, tt, fc * 128:(fc + 1) * 128],
                dst_fn=lambda tt, fc: xx_sb[:, fc, tt * 128:(tt + 1) * 128],
                nw_scale=None,
                extra_first_waits=[(S.bmm, S.bmm.n), (S.ye, S.ye.n)],
                per_tt_waits=rld_tt)
            b2_pass1(0, x2T_ready=(S.bte.n, S.bts.n))
            b2_pass2(0)
            b2_ln2out(0)

        @block.sync
        def _(sy):
            sy.wait_ge(S.out.h, S.out.n)

        @block.gpsimd
        def _(g):
            g.wait_ge(S.out.h, S.out.n)

        pb.close()
        pb_psum.close()

    return nc


def _host_prep(inputs):
    x = np.asarray(inputs["L0_attn_hidden_states"], dtype=np.float32)
    qkv_w = np.asarray(inputs["L0_attn_qkv_w"], dtype=np.float32)
    out_w = np.asarray(inputs["L0_attn_out_w"], dtype=np.float32)
    gate_w = np.asarray(inputs["L0_attn_gate_w"], dtype=np.float32)
    norm_w = np.asarray(inputs["L0_attn_norm_w"], dtype=np.float32)
    W = np.asarray(inputs["L0_ffn_W"], dtype=np.float32)
    b = np.asarray(inputs["L0_ffn_b"], dtype=np.float32)
    A = np.asarray(inputs["L0_ffn_A"], dtype=np.float32)
    Bmat = np.asarray(inputs["L0_ffn_Bmat"], dtype=np.float32)
    LiMEs = np.asarray(inputs["L0_ffn_LiMEs"], dtype=np.float32)

    xf = np.ascontiguousarray(x.reshape(8192, HID))
    xT = [np.ascontiguousarray(xf[bb * 4096:(bb + 1) * 4096].T) for bb in range(2)]
    qkvT = np.ascontiguousarray(qkv_w.T)

    def tile_w(wT):
        return np.ascontiguousarray(
            wT.reshape(32, 128, 16, 256).transpose(2, 0, 1, 3)
              .reshape(16, 32, 128, 256).transpose(0, 2, 1, 3))

    gateT = tile_w(np.ascontiguousarray(gate_w.T))
    outT = tile_w(np.ascontiguousarray(out_w.T))
    WTm_flat = np.ascontiguousarray(W.T)
    WTm = tile_w(WTm_flat)
    nwcol_h = np.ascontiguousarray(norm_w.reshape(32, 128).T)
    BmB_h = np.ascontiguousarray(np.concatenate([Bmat, b[None, :]], axis=0))
    b8v = np.ascontiguousarray(b[:8, None])
    limesT = np.ascontiguousarray(LiMEs.T)

    HEADS = 32
    base = 1.0 / 2.0 ** (8.0 / HEADS)
    factor = 1.0 - 1.0 / (31 + 1e-5) + 1e-5
    slope = np.array([base ** (h + 1) * factor for h in range(HEADS)], dtype=np.float64)
    r = np.arange(256, dtype=np.float64) + 1.0
    in_maps = []
    for c in range(NCORES):
        bb = c // 4
        heads = [8 * (c % 4) + i for i in range(8)]
        sl = slope[heads]
        qd = np.exp(-sl[:, None] * r[None, :])
        kd = np.exp(-sl[:, None] * (256.0 - r)[None, :])
        bd = np.exp(-sl * 256.0)
        diff = r[:, None] - r[None, :]
        dg = np.exp(-np.minimum(sl[:, None, None] * diff[None, :, :], 745.0))
        dg = np.where(diff[None, :, :] >= 0, dg, 0.0)
        dgT = dg.transpose(0, 2, 1)
        diag_in = np.ascontiguousarray(
            dgT.reshape(8, 2, 128, 256).transpose(2, 0, 1, 3).astype(np.float32))
        qdec_in = np.ascontiguousarray(
            qd.reshape(8, 2, 128).transpose(2, 0, 1).reshape(128, 16).astype(np.float32))
        kdec_in = np.ascontiguousarray(
            kd.reshape(8, 2, 128).transpose(2, 0, 1).reshape(128, 16).astype(np.float32))
        bdec_in = np.ascontiguousarray(
            np.broadcast_to(bd[None, :], (128, 8)).astype(np.float32))
        s0 = 512 * c
        in_maps.append({
            "xT_b": xT[bb],
            "xTloc": np.ascontiguousarray(
                np.concatenate([xT[0][:, s0:s0 + 512], xT[1][:, s0:s0 + 512]], axis=1)),
            "x_rows": np.ascontiguousarray(
                np.concatenate([xf[s0:s0 + 512], xf[4096 + s0:4096 + s0 + 512]], axis=0)),
            "qkvwT": np.ascontiguousarray(qkvT[:, 3072 * (c % 4):3072 * (c % 4) + 3072]),
            "gatewT": gateT,
            "outwT": outT,
            "WT": WTm,
            "Amat": A,
            "Bm": BmB_h,
            "limes": LiMEs,
            "limesT": limesT,
            "nwcol": nwcol_h,
            "b8": b8v,
            "diag": diag_in,
            "qdec": qdec_in,
            "kdec": kdec_in,
            "bdec": bdec_in,
        })
    return in_maps


def kernel(**inputs):
    in_maps = _host_prep(inputs)
    nc = build_nc(stage="full")
    res = run_bass_kernel_spmd(nc, in_maps, core_ids=list(range(NCORES)))
    full = np.empty((2, 4096, HID), dtype=np.float32)
    for c in range(NCORES):
        o = np.asarray(res.results[c]["out"])
        full[0, 512 * c:512 * c + 512] = o[0:512]
        full[1, 512 * c:512 * c + 512] = o[512:1024]
    return full
